# version 02
# baseline (speedup 1.0000x reference)
import math
import numpy as np
from contextlib import ExitStack

import concourse.bass as bass
import concourse.mybir as mybir
from concourse.bass_utils import run_bass_kernel_spmd

F32 = mybir.dt.float32
BF16 = mybir.dt.bfloat16
AF = mybir.ActivationFunctionType
ALU = mybir.AluOpType
AX = mybir.AxisListType

D = 2048
S = 2048
KF = 16
NH = 16
HD = 128
NB = 8
HALF = 1024
QSCALE = 1.0 / math.sqrt(HD)
EPS = 1e-5
NUNITS = 20
NPAIR = 11
DNK = 11
WSLOTS = 3
DSLOTS = 3
XSLOTS = 3

ENGINES = ("sync", "gpsimd", "vector", "scalar", "tensor")
MARKS = []


class Ctr:
    __slots__ = ("h", "n", "eng")

    def __init__(self, h, eng=None):
        self.h = h
        self.n = 0
        self.eng = eng


class Em:

    def __init__(self):
        self.fns = {e: [] for e in ENGINES}
        self.waited = {e: {} for e in ENGINES}
        self.marks = {e: [] for e in ENGINES}

    def wait(self, eng, ctr, val):
        if ctr is None or val is None or val <= 0:
            return
        if ctr.eng == eng:
            return
        w = self.waited[eng]
        k = id(ctr)
        if w.get(k, 0) >= val:
            return
        w[k] = val
        self.fns[eng].append(lambda E, c=ctr, v=val: E.wait_ge(c.h, v))

    def op(self, eng, fn, waits=(), inc=None, inc_by=1):
        for cv in waits:
            if cv is None:
                continue
            self.wait(eng, cv[0], cv[1])
        if inc is not None:
            inc.n += inc_by
            self.marks[eng].append((inc.n, fn.__code__.co_firstlineno))
            self.fns[eng].append(
                lambda E, f=fn, c=inc, v=inc_by: f(E).then_inc(c.h, v))
            return inc.n
        self.fns[eng].append(fn)
        return None


EM_LAST = [None]


def build_nc(taps=()):
    nc = bass.Bass("TRN2", target_bir_lowering=False, debug=False)

    x_fm = nc.dram_tensor("x_fm", [D, S], F32, kind="ExternalInput").ap()
    aw = nc.dram_tensor("aw", [5 * NH + KF, 128, 2048], F32, kind="ExternalInput").ap()
    up_all = nc.dram_tensor("up_all", [NUNITS, 2 * NPAIR, 128, 2048], F32,
                            kind="ExternalInput").ap()
    dn_all = nc.dram_tensor("dn_all", [NUNITS, KF, 128, DNK * 128], F32,
                            kind="ExternalInput").ap()
    gate_t = nc.dram_tensor("gate_t", [128, KF * 16], F32, kind="ExternalInput").ap()
    sg_t = nc.dram_tensor("sg_t", [128, KF], F32, kind="ExternalInput").ap()
    lam_t = nc.dram_tensor("lam_t", [1, NH], F32, kind="ExternalInput").ap()
    x2s = nc.dram_tensor("x2s", [D, S], F32, kind="Internal").ap()
    coefT_d = nc.dram_tensor("coefT_d", [128, 128], BF16, kind="Internal").ap()
    out_fm = nc.dram_tensor("out_fm", [D, S], F32, kind="ExternalOutput").ap()
    tap_t = {}
    for name, shape, dt in taps:
        tap_t[name] = nc.dram_tensor("tap_" + name, list(shape), dt,
                                     kind="ExternalOutput").ap()

    em = Em()
    EM_LAST[0] = em
    with ExitStack() as ctx:
        sb = lambda nm, s, d: ctx.enter_context(nc.sbuf_tensor(nm, s, d))
        psb = lambda nm: ctx.enter_context(nc.psum_tensor(nm, [128, 512], F32))
        mksem = lambda nm, eng=None: Ctr(ctx.enter_context(nc.semaphore(nm)), eng)

        sa = sb("sa", [128, KF * 2048], BF16)
        ot = sb("ot", [128, KF * 2048], BF16)
        qkv = sb("qkv", [128, 8192], BF16)
        v_sb = sb("v_sb", [128, 2048], BF16)
        sc1 = sb("sc1", [128, 2048], BF16)
        sc2 = sb("sc2", [128, 2048], BF16)
        sc3 = sb("sc3", [128, 2048], BF16)
        pct = sb("pct", [128, 2048], BF16)
        wring = sb("wring", [128, WSLOTS * 2048], BF16)
        dnring = sb("dnring", [128, DSLOTS * DNK * 128], BF16)
        xring = sb("xring", [128, XSLOTS * 512], F32)
        bc0 = sb("bc0", [128, 2048], BF16)
        bc1 = sb("bc1", [128, 2048], BF16)
        cbc = sb("cbc", [128, 1024], BF16)
        sgbc = sb("sgbc", [128, 1024], BF16)
        gate_sb = sb("gate_sb", [128, KF * 16], F32)
        sg_sb = sb("sg_sb", [128, KF], BF16)
        lam_bc = sb("lam_bc", [128, NH], F32)
        ident = sb("ident", [128, 128], BF16)
        ones_row = sb("ones_row", [1, 128], BF16)
        ones_col = sb("ones_col", [128, 1], BF16)
        statcol = sb("statcol", [128, 32], F32)
        statrow = sb("statrow", [1, 16], F32)
        lamrow = sb("lamrow", [1, NH], F32)
        lamrow_bf = sb("lamrow_bf", [1, NH], BF16)
        nbias_bf = sb("nbias_bf", [1, 2], BF16)
        rowstage_t = sb("rowstage_t", [1, 1024], BF16)

        sa_t = lambda k: sa[:, k * 2048:(k + 1) * 2048]
        ot_t = lambda k: ot[:, k * 2048:(k + 1) * 2048]
        acc_t = lambda k: ot[:, k * 2048:(k + 1) * 2048].bitcast(F32)
        Q1 = qkv[:, 0:2048]
        K1 = qkv[:, 2048:4096]
        Q2 = qkv[:, 4096:6144]
        K2 = qkv[:, 6144:8192]
        f32tmp = qkv[:, 0:1024].bitcast(F32)
        rowA = qkv[0:1, 0:4096].bitcast(F32)
        rowB = qkv[0:1, 4096:8192].bitcast(F32)
        arow = sc1[0:1, :]
        brow = sc2[0:1, :]
        rl_f = pct[:, 0:256].bitcast(F32)
        rank_f = pct[:, 256:512].bitcast(F32)
        gtmp_f = pct[:, 512:768].bitcast(F32)
        coefbf = pct[:, 768:896]
        coefT = pct[:, 896:1024]
        rowstage = rowstage_t[0:1, :]
        sgrow = rowstage
        iota_f = sc1[:, 0:256].bitcast(F32)

        wslot = lambda i: wring[:, (i % WSLOTS) * 2048:(i % WSLOTS) * 2048 + 2048]
        dslot = lambda i: dnring[:, (i % DSLOTS) * 1408:(i % DSLOTS) * 1408 + 1408]
        xslot = lambda i: xring[:, (i % XSLOTS) * 512:(i % XSLOTS) * 512 + 512]

        B = [psb("pb%d" % i) for i in range(8)]

        pe = mksem("pe", "tensor")
        ve = mksem("ve", "vector")
        se = mksem("se", "scalar")
        gp = mksem("gp", "gpsimd")
        rv = mksem("rv")
        dG = mksem("dG")
        dO = mksem("dO")
        s_lam = mksem("s_lam")
        s_gate = mksem("s_gate")
        s_sgw = mksem("s_sgw")
        wsem = [mksem("wsem%d" % i) for i in range(WSLOTS)]
        dsem = [mksem("dsem%d" % i) for i in range(DSLOTS)]
        xsem = [mksem("xsem%d" % i) for i in range(XSLOTS)]

        block = ctx.enter_context(nc.Block())

        T = lambda fn, waits=(), inc=pe: em.op("tensor", fn, waits, inc)
        V = lambda fn, waits=(), inc=ve: em.op("vector", fn, waits, inc)
        Sc = lambda fn, waits=(), inc=se: em.op("scalar", fn, waits, inc)
        G = lambda fn, waits=(), inc=None, inc_by=16: em.op("gpsimd", fn, waits, inc, inc_by)
        Sy = lambda fn, waits=(), inc=None, inc_by=16: em.op("sync", fn, waits, inc, inc_by)

        def tap(name, src_ap, waits=()):
            if name in tap_t:
                Sy(lambda E, o=tap_t[name], i=src_ap: E.dma_start(out=o, in_=i),
                   waits, inc=dO)

        bfree = {b: None for b in range(8)}
        m_pc_last = None

        m_iota = G(lambda E: E.iota(iota_f, pattern=[[1, 128]], base=0,
                                    channel_multiplier=-1,
                                    allow_small_or_imprecise_dtypes=True),
                   inc=gp, inc_by=1)
        V(lambda E: E.memset(ones_row[:], 1.0))
        V(lambda E: E.memset(ones_col[:], 1.0))
        m_id = V(lambda E: E.tensor_single_scalar(ident[:], iota_f, 0.0, ALU.is_equal),
                 [(gp, m_iota)])
        m_lam_in = Sy(lambda E: E.dma_start(out=lamrow[:], in_=lam_t), inc=s_lam)
        m_lamb = V(lambda E: E.tensor_copy(lamrow_bf[:], lamrow[:]), [(s_lam, m_lam_in)])
        m_lamm = T(lambda E: E.matmul(B[0][:, 0:NH], ones_row[:], lamrow_bf[:],
                                      start=True, stop=True), [(ve, m_lamb)])
        m_lam = V(lambda E: E.tensor_copy(lam_bc[:], B[0][:, 0:NH]), [(pe, m_lamm)])
        bfree[0] = (ve, m_lam)
        m_gate = G(lambda E: E.dma_start(out=gate_sb[:], in_=gate_t), inc=s_gate)
        m_sgw = G(lambda E: E.dma_start(out=sg_sb[:], in_=sg_t), inc=s_sgw)

        def stat_mm(nq, c, nchunks, sum_rhs, sq_rhs, extra_waits=()):
            st = (c == 0)
            sp = (c == nchunks - 1)
            T(lambda E, n=nq, r=sum_rhs, st=st, sp=sp:
              E.matmul(B[n][0:1, :], ones_col[:], r, start=st, stop=sp),
              extra_waits)
            T(lambda E, n=nq, r=sq_rhs, st=st, sp=sp:
              E.matmul(B[4 + n][0:1, :], ones_col[:], r, start=st, stop=sp))

        def ln_finalize(nquart, bc_lo):
            m_stats = pe.n
            for n in range(nquart):
                V(lambda E, n=n: E.tensor_scalar(
                    rowA[0:1, n * 512:(n + 1) * 512], B[n][0:1, :],
                    1.0 / D, None, ALU.mult), [(pe, m_stats)])
                V(lambda E, n=n: E.tensor_scalar(
                    rowB[0:1, n * 512:(n + 1) * 512], B[4 + n][0:1, :],
                    1.0 / D, None, ALU.mult))
            for n in range(nquart):
                V(lambda E, n=n: E.tensor_tensor(
                    B[n][0:1, :], rowA[0:1, n * 512:(n + 1) * 512],
                    rowA[0:1, n * 512:(n + 1) * 512], ALU.mult))
                V(lambda E, n=n: E.tensor_tensor(
                    rowB[0:1, n * 512:(n + 1) * 512],
                    rowB[0:1, n * 512:(n + 1) * 512], B[n][0:1, :], ALU.subtract))
            w = nquart * 512
            V(lambda E, w=w: E.tensor_scalar(rowB[0:1, 0:w], rowB[0:1, 0:w],
                                             EPS, None, ALU.add))
            m_var = ve.n
            Sc(lambda E, w=w: E.activation(rowB[0:1, 0:w], rowB[0:1, 0:w], AF.Ln,
                                           bias=0.0, scale=1.0), [(ve, m_var)])
            m_rstd = Sc(lambda E, w=w: E.activation(rowB[0:1, 0:w], rowB[0:1, 0:w],
                                                    AF.Exp, bias=0.0, scale=-0.5))
            V(lambda E, w=w: E.tensor_tensor(brow[0:1, 0:w], rowA[0:1, 0:w],
                                             rowB[0:1, 0:w], ALU.mult),
              [(se, m_rstd)])
            m_arow = V(lambda E, w=w: E.tensor_copy(arow[0:1, 0:w], rowB[0:1, 0:w]))
            for n in range(nquart):
                T(lambda E, n=n: E.matmul(B[n][:, :], ones_row[:],
                                          arow[0:1, n * 512:(n + 1) * 512],
                                          start=True, stop=True), [(ve, m_arow)])
                T(lambda E, n=n: E.matmul(B[4 + n][:, :], ones_row[:],
                                          brow[0:1, n * 512:(n + 1) * 512],
                                          start=True, stop=True))
            m_bcmm = pe.n
            m_last = None
            for n in range(nquart):
                V(lambda E, n=n, lo=bc_lo: E.tensor_copy(
                    bc0[:, lo + n * 512:lo + (n + 1) * 512], B[n][:, :]),
                    [(pe, m_bcmm)])
                m_last = V(lambda E, n=n, lo=bc_lo: E.tensor_copy(
                    bc1[:, lo + n * 512:lo + (n + 1) * 512], B[4 + n][:, :]))
            for n in range(nquart):
                bfree[n] = (ve, m_last)
                bfree[4 + n] = (ve, m_last)
            return m_last

        xuse = [0]
        xslot_free = [[] for _ in range(XSLOTS)]

        def next_xslot():
            i = xuse[0] % XSLOTS
            xuse[0] += 1
            return i, list(xslot_free[i])

        MARKS.append(("ln1", pe.n, ve.n, se.n, dG.n))
        m_pstat = 0
        for c in range(KF):
            m_pstat_next = pe.n
            for q in range(4):
                i, fr = next_xslot()
                m_in = Sy(lambda E, i=i, c=c, q=q: E.dma_start(
                    out=xslot(i),
                    in_=x_fm[c * 128:(c + 1) * 128, q * 512:(q + 1) * 512]), fr,
                    inc=xsem[i])
                m_cast = V(lambda E, i=i, c=c, q=q: E.tensor_copy(
                    sa_t(c)[:, q * 512:(q + 1) * 512], xslot(i)), [(xsem[i], m_in)])
                m_sq = Sc(lambda E, i=i, q=q: E.activation(
                    sc3[:, q * 512:(q + 1) * 512], xslot(i), AF.Square),
                    [(xsem[i], m_in), (pe, m_pstat)])
                xslot_free[i] = [(ve, m_cast), (se, m_sq)]
                ew = [(ve, m_cast), (se, m_sq)]
                if c == 0:
                    for bq in (q, 4 + q):
                        if bfree[bq]:
                            ew.append(bfree[bq])
                stat_mm(q, c, KF,
                        sa_t(c)[:, q * 512:(q + 1) * 512],
                        sc3[:, q * 512:(q + 1) * 512], ew)
            m_pstat = pe.n
        m_ln1bc = ln_finalize(4, 0)

        for c in range(KF):
            for q in range(4):
                i, fr = next_xslot()
                m_in = Sy(lambda E, i=i, c=c, q=q: E.dma_start(
                    out=xslot(i),
                    in_=x_fm[c * 128:(c + 1) * 128, q * 512:(q + 1) * 512]), fr,
                    inc=xsem[i])
                V(lambda E, i=i, q=q: E.tensor_tensor(
                    f32tmp[:, :], xslot(i), bc0[:, q * 512:(q + 1) * 512], ALU.mult),
                    [(xsem[i], m_in), (ve, m_ln1bc)])
                m_ap = V(lambda E, c=c, q=q: E.tensor_tensor(
                    sa_t(c)[:, q * 512:(q + 1) * 512], f32tmp[:, :],
                    bc1[:, q * 512:(q + 1) * 512], ALU.subtract))
                xslot_free[i] = [(ve, m_ap)]
        m_xn1 = ve.n
        tap("xn1", sa[:, 0:4096], [(ve, m_xn1)])

        NSLAB_A = 5 * NH + KF
        slab_done = {}
        slab_arr = {}
        issued = [0]

        def ensure_slab(s):
            while issued[0] <= min(s, NSLAB_A - 1):
                si = issued[0]
                prev = si - WSLOTS
                w = [slab_done[prev]] if prev >= 0 else []
                m = G(lambda E, si=si: E.dma_start(
                    out=wslot(si), in_=aw[si]), w, inc=wsem[si % WSLOTS])
                slab_arr[si] = (wsem[si % WSLOTS], m)
                issued[0] += 1

        ensure_slab(WSLOTS - 1)

        MARKS.append(("headloop", pe.n, ve.n, se.n, dG.n))
        bfree = {b: None for b in range(8)}

        for h in range(NH):
            MARKS.append((f"head{h}", pe.n, ve.n, se.n, dG.n))
            for wi, (dst, scale) in enumerate(
                    [(Q1, QSCALE), (K1, 1.0), (Q2, QSCALE), (K2, 1.0)]):
                s = 5 * h + wi
                ensure_slab(s + WSLOTS - 1)
                for n in range(4):
                    b = n % 2
                    for k in range(KF):
                        waits = []
                        if k == 0:
                            waits = [slab_arr[s], (ve, m_xn1)]
                            if bfree[b]:
                                waits.append(bfree[b])
                        T(lambda E, b=b, s=s, k=k, n=n, st=(k == 0), sp=(k == KF - 1):
                          E.matmul(B[b][:, :], wslot(s)[:, k * 128:(k + 1) * 128],
                                   sa_t(k)[:, n * 512:(n + 1) * 512],
                                   start=st, stop=sp), waits)
                    m_chain = pe.n
                    m_ev = Sc(lambda E, dst=dst, n=n, b=b, scale=scale: E.activation(
                        dst[:, n * 512:(n + 1) * 512], B[b][:, :], AF.Copy,
                        bias=0.0, scale=scale), [(pe, m_chain)])
                    bfree[b] = (se, m_ev)
                slab_done[s] = (pe, pe.n)

            s = 5 * h + 4
            ensure_slab(s + WSLOTS - 1)
            vwave = {}
            for m in range(16):
                b = m % 2
                off = ((m // 2) % 4) * 128
                for k in range(KF):
                    waits = []
                    if k == 0:
                        waits = [slab_arr[s]]
                        if m <= 1 and bfree[b]:
                            waits.append(bfree[b])
                        if m >= 2:
                            waits.append(vwave[m - 2])
                        if m >= 8:
                            waits.append(vwave[m - 8])
                    T(lambda E, b=b, off=off, s=s, k=k, m=m,
                      st=(k == 0), sp=(k == KF - 1):
                      E.matmul(B[b][:, off:off + 128],
                               sa_t(k)[:, m * 128:(m + 1) * 128],
                               wslot(s)[:, k * 128:(k + 1) * 128],
                               start=st, stop=sp), waits)
                m_chain = pe.n
                m_ev = Sc(lambda E, m=m, b=b, off=off: E.activation(
                    v_sb[:, m * 128:(m + 1) * 128], B[b][:, off:off + 128],
                    AF.Copy, bias=0.0, scale=1.0), [(pe, m_chain)])
                vwave[m] = (se, m_ev)
            bfree[0] = vwave[14]
            bfree[1] = vwave[15]
            slab_done[s] = (pe, pe.n)

            rbanks = [2, 3, 6, 7]
            m_bmm = 0
            for mi, (QA, KA) in enumerate([(Q1, K1), (Q2, K2)]):
                for xi, XA in enumerate([QA, KA]):
                    sqw = [(pe, m_bmm)]
                    if m_pc_last:
                        sqw.append((ve, m_pc_last))
                    m_sq = Sc(lambda E, XA=XA: E.activation(sc3[:, :], XA, AF.Square),
                              sqw)
                    for n in range(4):
                        waits = [(se, m_sq)]
                        if bfree[rbanks[n]]:
                            waits.append(bfree[rbanks[n]])
                        T(lambda E, n=n: E.matmul(
                            B[rbanks[n]][0:1, :], ones_col[:],
                            sc3[:, n * 512:(n + 1) * 512], start=True, stop=True),
                            waits)
                    m_mm = pe.n
                    m_bmm = m_mm
                    m_r4 = None
                    for n in range(4):
                        m_r4 = V(lambda E, n=n, xi=xi: E.reduce_max(
                            statrow[0:1, 4 * xi + n:4 * xi + n + 1],
                            B[rbanks[n]][0:1, :], axis=AX.X), [(pe, m_mm)],
                            inc=rv if n == 3 else ve)
                    m_nmax = V(lambda E, xi=xi: E.reduce_max(
                        statrow[0:1, 8 + xi:9 + xi], statrow[0:1, 4 * xi:4 * xi + 4],
                        axis=AX.X), [(rv, m_r4)], inc=rv)
                    em.wait("vector", rv, m_nmax)
                m_b2 = V(lambda E: E.tensor_tensor(
                    statrow[0:1, 10:11], statrow[0:1, 8:9],
                    statrow[0:1, 9:10], ALU.mult), [(rv, m_nmax)])
                m_ln = Sc(lambda E: E.activation(statrow[0:1, 10:11],
                                                 statrow[0:1, 10:11],
                                                 AF.Ln, bias=0.0, scale=1.0),
                          [(ve, m_b2)])
                V(lambda E: E.tensor_scalar(
                    statrow[0:1, 10:11], statrow[0:1, 10:11], 0.5,
                    math.log(1.02 / math.sqrt(HD)), ALU.mult, ALU.add),
                  [(se, m_ln)])
                m_vb = ve.n
                m_b = Sc(lambda E: E.activation(
                    statrow[0:1, 10:11], statrow[0:1, 10:11], AF.Exp,
                    bias=0.0, scale=1.0), [(ve, m_vb)])
                m_nb = V(lambda E, mi=mi: E.tensor_scalar(
                    nbias_bf[0:1, mi:mi + 1], statrow[0:1, 10:11], -1.0, None,
                    ALU.mult), [(se, m_b)])
                waits = [(ve, m_nb)]
                if bfree[2]:
                    waits.append(bfree[2])
                T(lambda E, mi=mi: E.matmul(
                    B[2][:, 256 + mi:257 + mi], ones_row[:],
                    nbias_bf[0:1, mi:mi + 1], start=True, stop=True), waits)
                m_bb = pe.n
                V(lambda E, mi=mi: E.tensor_copy(
                    statcol[:, 8 + mi:9 + mi], B[2][:, 256 + mi:257 + mi]),
                    [(pe, m_bb)])
            m_biases = ve.n
            if h == 0:
                tap("q1h0", Q1, [(se, se.n)])
                tap("vh0", v_sb[:], [(se, se.n)])

            def emit_scores(qt):
                par = qt % 2
                zoff = 16 * par
                E1t = sc1 if par == 0 else sc3
                E2t = sc2 if par == 0 else bc0
                for (E_sb, zlo, b0, bias_ix, lhsQ, rhsK) in (
                        (E1t, zoff + 0, 2, 8, Q1, K1), (E2t, zoff + 4, 4, 9, Q2, K2)):
                    for n in range(4):
                        b = b0 + (n % 2)
                        waits = []
                        if bfree[b]:
                            waits.append(bfree[b])
                        if qt == 0:
                            waits.append((ve, m_biases))
                        T(lambda E, b=b, lhsQ=lhsQ, rhsK=rhsK, qt=qt, n=n:
                          E.matmul(B[b][:, :], lhsQ[:, qt * 128:(qt + 1) * 128],
                                   rhsK[:, n * 512:(n + 1) * 512],
                                   start=True, stop=True), waits)
                        m_s = pe.n
                        m_e = Sc(lambda E, E_sb=E_sb, n=n, b=b, zlo=zlo, bx=bias_ix:
                                 E.activation(E_sb[:, n * 512:(n + 1) * 512],
                                              B[b][:, :], AF.Exp,
                                              bias=statcol[:, bx:bx + 1], scale=1.0,
                                              accum_out=statcol[:, zlo + n:zlo + n + 1]),
                                 [(pe, m_s)])
                        bfree[b] = (se, m_e)
                m_exps = se.n
                V(lambda E, zoff=zoff: E.reduce_sum(
                    statcol[:, zoff + 12:zoff + 13], statcol[:, zoff:zoff + 4],
                    axis=AX.X), [(se, m_exps)])
                m_z2 = V(lambda E, zoff=zoff: E.reduce_sum(
                    statcol[:, zoff + 13:zoff + 14], statcol[:, zoff + 4:zoff + 8],
                    axis=AX.X), inc=rv)
                V(lambda E, zoff=zoff: E.reciprocal(
                    statcol[:, zoff + 14:zoff + 15], statcol[:, zoff + 12:zoff + 13]),
                  [(rv, m_z2)])
                m_rv = V(lambda E, zoff=zoff: E.reciprocal(
                    statcol[:, zoff + 15:zoff + 16], statcol[:, zoff + 13:zoff + 14]),
                    inc=rv)
                V(lambda E, h=h, zoff=zoff: E.tensor_tensor(
                    statcol[:, zoff + 15:zoff + 16], statcol[:, zoff + 15:zoff + 16],
                    lam_bc[:, h:h + 1], ALU.mult), [(rv, m_rv), (ve, m_lam)])
                for n in range(4):
                    V(lambda E, E1t=E1t, n=n, zoff=zoff: E.tensor_scalar(
                        E1t[:, n * 512:(n + 1) * 512], E1t[:, n * 512:(n + 1) * 512],
                        statcol[:, zoff + 14:zoff + 15], None, ALU.mult))
                    m_pcn = V(lambda E, E1t=E1t, E2t=E2t, n=n, zoff=zoff:
                              E.scalar_tensor_tensor(
                        E2t[:, n * 512:(n + 1) * 512], E2t[:, n * 512:(n + 1) * 512],
                        statcol[:, zoff + 15:zoff + 16],
                        E1t[:, n * 512:(n + 1) * 512], ALU.mult, ALU.subtract))
                    pc_marks[qt * 4 + n] = m_pcn
                return m_pcn

            def emit_tpv(qt):
                par = qt % 2
                E2t = sc2 if par == 0 else bc0
                pct_evs = []
                for wv in range(4):
                    b = 6 + (wv % 2)
                    for j in range(4):
                        jj = wv * 4 + j
                        waits = []
                        if j == 0:
                            waits = [(ve, pc_marks[qt * 4 + wv])]
                            if bfree[b]:
                                waits.append(bfree[b])
                        T(lambda E, b=b, jj=jj, j=j, E2t=E2t: E.matmul(
                            B[b][:, j * 128:(j + 1) * 128],
                            E2t[:, jj * 128:(jj + 1) * 128], ident[:],
                            start=True, stop=True), waits)
                    m_t = pe.n
                    m_evt = V(lambda E, b=b, wv=wv: E.tensor_copy(
                        pct[:, wv * 512:(wv + 1) * 512], B[b][:, :]), [(pe, m_t)])
                    bfree[b] = (ve, m_evt)
                    pct_evs.append((ve, m_evt))
                for j in range(16):
                    waits = []
                    if j == 0:
                        waits = list(pct_evs)
                        if bfree[6]:
                            waits.append(bfree[6])
                    T(lambda E, j=j, st=(j == 0), sp=(j == 15): E.matmul(
                        B[6][:, 0:128], v_sb[:, j * 128:(j + 1) * 128],
                        pct[:, j * 128:(j + 1) * 128], start=st, stop=sp), waits)
                m_pv = pe.n
                m_pve = V(lambda E, h=h, qt=qt: E.tensor_scalar(
                    ot_t(h)[:, qt * 128:(qt + 1) * 128], B[6][:, 0:128],
                    -1.0, None, ALU.mult), [(pe, m_pv)])
                bfree[6] = (ve, m_pve)

            pc_marks = {}
            emit_scores(0)
            for qt in range(1, 16):
                emit_scores(qt)
                emit_tpv(qt - 1)
            emit_tpv(15)
            m_pc_last = pc_marks[15 * 4 + 3]
            if h == 0:
                tap("oth0", ot[:, 0:2048], [(ve, ve.n)])

        MARKS.append(("o_proj", pe.n, ve.n, se.n, dG.n))
        for m in range(16):
            s = 5 * NH + m
            ensure_slab(s + WSLOTS - 1)
            banks = [0, 1, 2, 3] if (m % 2 == 0) else [4, 5, 6, 7]
            for n in range(4):
                bq = banks[n]
                for k in range(KF):
                    waits = []
                    if k == 0:
                        waits = [slab_arr[s]]
                        if bfree[bq]:
                            waits.append(bfree[bq])
                    T(lambda E, b=bq, s=s, k=k, n=n, st=(k == 0), sp=(k == KF - 1):
                      E.matmul(B[b][:, :], wslot(s)[:, k * 128:(k + 1) * 128],
                               ot_t(k)[:, n * 512:(n + 1) * 512],
                               start=st, stop=sp), waits)
            m_chain = pe.n
            for n in range(4):
                m_ev = V(lambda E, m=m, n=n, b=banks[n]: E.tensor_copy(
                    sa_t(m)[:, n * 512:(n + 1) * 512], B[b][:, :]), [(pe, m_chain)])
                bfree[banks[n]] = (ve, m_ev)
            slab_done[s] = (pe, pe.n)
        m_ao = ve.n
        tap("ao", sa[:, 0:4096], [(ve, m_ao)])

        MARKS.append(("ln2", pe.n, ve.n, se.n, dG.n))
        m_pstat = 0
        for c in range(KF):
            waits = [(ve, m_ao), (pe, m_pstat)]
            m_sq = Sc(lambda E, c=c: E.activation(sc3[:, :], sa_t(c), AF.Square),
                      waits)
            for q in range(4):
                ew = [(se, m_sq)]
                if c == 0:
                    for bq in (q, 4 + q):
                        if bfree[bq]:
                            ew.append(bfree[bq])
                stat_mm(q, c, KF, sa_t(c)[:, q * 512:(q + 1) * 512],
                        sc3[:, q * 512:(q + 1) * 512], ew)
            m_pstat = pe.n
        m_ln2bc = ln_finalize(4, 0)

        MARKS.append(("x2pass", pe.n, ve.n, se.n, dG.n))
        m_pstat = 0
        for c in range(KF):
            m_pstat_u = m_pstat
            for q in range(4):
                i, fr = next_xslot()
                m_in = Sy(lambda E, i=i, c=c, q=q: E.dma_start(
                    out=xslot(i),
                    in_=x_fm[c * 128:(c + 1) * 128, q * 512:(q + 1) * 512]), fr,
                    inc=xsem[i])
                V(lambda E, c=c, q=q: E.tensor_tensor(
                    f32tmp[:, :], sa_t(c)[:, q * 512:(q + 1) * 512],
                    bc0[:, q * 512:(q + 1) * 512], ALU.mult), [(ve, m_ln2bc)])
                V(lambda E, q=q: E.tensor_tensor(
                    f32tmp[:, :], f32tmp[:, :], bc1[:, q * 512:(q + 1) * 512],
                    ALU.subtract))
                m_x2 = V(lambda E, i=i: E.tensor_tensor(
                    xslot(i), f32tmp[:, :], xslot(i), ALU.add), [(xsem[i], m_in)])
                m_wr = Sy(lambda E, i=i, c=c, q=q: E.dma_start(
                    out=x2s[c * 128:(c + 1) * 128, q * 512:(q + 1) * 512],
                    in_=xslot(i)), [(ve, m_x2)], inc=xsem[i])
                m_cast = V(lambda E, i=i, c=c, q=q: E.tensor_copy(
                    sa_t(c)[:, q * 512:(q + 1) * 512], xslot(i)))
                m_sq = Sc(lambda E, i=i, q=q: E.activation(
                    sc3[:, q * 512:(q + 1) * 512], xslot(i), AF.Square),
                    [(ve, m_x2), (pe, m_pstat_u)])
                xslot_free[i] = [(xsem[i], m_wr), (ve, m_cast), (se, m_sq)]
                stat_mm(q, c, KF, sa_t(c)[:, q * 512:(q + 1) * 512],
                        sc3[:, q * 512:(q + 1) * 512],
                        [(ve, m_cast), (se, m_sq)])
            m_pstat = pe.n
        m_ln3bc = ln_finalize(4, 0)
        x2done = [(xsem[i], xsem[i].n) for i in range(XSLOTS)]

        up_issued = [issued[0]]
        up_arr = dict(slab_arr)
        up_done = dict(slab_done)

        def issue_up(src_ap):
            si = up_issued[0]
            prev = si - WSLOTS
            if prev >= 0:
                assert prev in up_done, f"up ring reuse without mark: {si}"
            w = [up_done[prev]] if prev >= 0 and prev in up_done else []
            m = G(lambda E, si=si, s=src_ap: E.dma_start(
                out=wslot(si), in_=s), w, inc=wsem[si % WSLOTS])
            up_arr[si] = (wsem[si % WSLOTS], m)
            up_issued[0] += 1
            return si

        dn_issued = [0]
        dn_arr = {}
        dn_done = {}

        def issue_dn(src_ap):
            si = dn_issued[0]
            prev = si - DSLOTS
            if prev >= 0:
                assert prev in dn_done, f"dn ring reuse without mark: {si}"
                w = [dn_done[prev]]
            else:
                w = []
            m = G(lambda E, si=si, s=src_ap: E.dma_start(
                out=dslot(si), in_=s), w, inc=dsem[si % DSLOTS])
            dn_arr[si] = (dsem[si % DSLOTS], m)
            dn_issued[0] += 1
            return si

        m_rtmm = None
        m_final_prev = None
        m_units_pe_prev = None
        m_dn_all_prev = None
        m_acc_prev = None

        for half in range(2):
            t0 = half * HALF
            MARKS.append((f"half{half}", pe.n, ve.n, se.n, dG.n))
            for c in range(KF):
                for q2 in range(2):
                    i, fr = next_xslot()
                    gq = t0 + q2 * 512
                    m_in = Sy(lambda E, i=i, c=c, gq=gq: E.dma_start(
                        out=xslot(i), in_=x2s[c * 128:(c + 1) * 128, gq:gq + 512]),
                        fr + x2done, inc=xsem[i])
                    waits = [(xsem[i], m_in), (ve, m_ln3bc)]
                    if m_units_pe_prev:
                        waits.append((pe, m_units_pe_prev))
                    if m_rtmm:
                        waits.append((pe, m_rtmm))
                    V(lambda E, i=i, gq=gq: E.tensor_tensor(
                        f32tmp[:, :], xslot(i), bc0[:, gq:gq + 512], ALU.mult),
                        waits)
                    m_ap = V(lambda E, gq=gq: E.tensor_tensor(
                        f32tmp[:, :], f32tmp[:, :],
                        bc1[:, gq:gq + 512], ALU.subtract))
                    m_cast3 = V(lambda E, c=c, q2=q2: E.tensor_copy(
                        sa_t(c)[:, q2 * 512:(q2 + 1) * 512], f32tmp[:, :]))
                    xslot_free[i] = [(ve, m_ap)]
                    for j in range(4):
                        tt = q2 * 4 + j
                        rwts = [(ve, m_ap)]
                        if c == 0:
                            rwts.append((s_gate, m_gate))
                            if bfree[tt]:
                                rwts.append(bfree[tt])
                        T(lambda E, c=c, j=j, tt=tt, st=(c == 0), sp=(c == KF - 1):
                          E.matmul(B[tt][:, 0:16],
                                   f32tmp[:, j * 128:(j + 1) * 128],
                                   gate_sb[:, c * 16:(c + 1) * 16],
                                   start=st, stop=sp), rwts)
                    m_rtmm = pe.n
            m_xn3 = ve.n
            if half == 0:
                tap("xn3", sa[:, 0:2048], [(ve, m_xn3)])

            MARKS.append((f"router{half}", pe.n, ve.n, se.n, dG.n))
            for tt in range(8):
                V(lambda E, tt=tt: E.tensor_copy(rl_f[:, tt * 16:(tt + 1) * 16],
                                                 B[tt][:, 0:16]), [(pe, m_rtmm)])
            m_rle0 = ve.n
            for tt in range(8):
                for k in range(KF):
                    waits = [(s_sgw, m_sgw)] if (tt == 0 and k == 0) else []
                    if tt == 0 and k == 0:
                        waits.append((ve, m_rle0))
                    T(lambda E, tt=tt, k=k, st=(k == 0), sp=(k == KF - 1): E.matmul(
                        B[6][:, 128 + tt:129 + tt],
                        sa_t(k)[:, tt * 128:(tt + 1) * 128],
                        sg_sb[:, k:k + 1], start=st, stop=sp), waits)
            m_sgl = pe.n
            m_rle = m_rle0
            m_sge = Sc(lambda E: E.activation(statcol[:, 0:8], B[6][:, 128:136],
                                              AF.Sigmoid), [(pe, m_sgl)])
            bfree[6] = (se, m_sge)

            V(lambda E: E.memset(rank_f, 0.0))
            for tt in range(8):
                sl = rl_f[:, tt * 16:(tt + 1) * 16]
                V(lambda E, sl=sl: E.reduce_max(statcol[:, 10:11], sl, axis=AX.X,
                                                negate=True), [(ve, m_rle)])
                m_ex = Sc(lambda E, sl=sl: E.activation(
                    sl, sl, AF.Exp, bias=statcol[:, 10:11], scale=1.0,
                    accum_out=statcol[:, 11:12]), [(ve, ve.n)])
                m_r = V(lambda E: E.reciprocal(statcol[:, 12:13], statcol[:, 11:12]),
                        [(se, m_ex)], inc=rv)
                m_rwv = V(lambda E, sl=sl: E.tensor_scalar(
                    sl, sl, statcol[:, 12:13], None, ALU.mult), [(rv, m_r)], inc=rv)
            m_rw = m_rwv
            rl3 = rl_f.rearrange("p (g e) -> p g e", e=16)
            rk3 = rank_f.rearrange("p (g e) -> p g e", e=16)
            gt3 = gtmp_f.rearrange("p (g e) -> p g e", e=16)
            for s_ in range(1, 16):
                w_ = 16 - s_
                m_g1 = V(lambda E, s_=s_, w_=w_: E.tensor_tensor(
                    gt3[:, :, 0:w_], rl3[:, :, s_:16], rl3[:, :, 0:w_], ALU.is_gt),
                    [(rv, m_rw)], inc=rv)
                V(lambda E, s_=s_, w_=w_: E.tensor_tensor(
                    rk3[:, :, 0:w_], rk3[:, :, 0:w_], gt3[:, :, 0:w_], ALU.add),
                    [(rv, m_g1)])
                m_g2 = V(lambda E, s_=s_, w_=w_: E.tensor_scalar(
                    gt3[:, :, 0:w_], gt3[:, :, 0:w_], -1.0, 1.0, ALU.mult, ALU.add),
                    inc=rv)
                m_g3 = V(lambda E, s_=s_, w_=w_: E.tensor_tensor(
                    rk3[:, :, s_:16], rk3[:, :, s_:16], gt3[:, :, 0:w_], ALU.add),
                    [(rv, m_g2)], inc=rv)
                em.wait("vector", rv, m_g3)
            m_rk = V(lambda E: E.tensor_single_scalar(rank_f, rank_f, 3.5, ALU.is_lt),
                     inc=rv)
            m_cf = V(lambda E: E.tensor_tensor(rank_f, rank_f, rl_f, ALU.mult),
                     [(rv, m_rk)], inc=rv)
            m_cbf = V(lambda E: E.tensor_copy(coefbf, rank_f), [(rv, m_cf)])
            waits = [(ve, m_cbf), (ve, m_id)]
            if bfree[7]:
                waits.append(bfree[7])
            T(lambda E: E.matmul(B[7][:, 0:128], coefbf, ident[:],
                                 start=True, stop=True), waits)
            m_ctm = pe.n
            m_ct = V(lambda E: E.tensor_copy(coefT, B[7][:, 0:128]), [(pe, m_ctm)])
            m_ctd = G(lambda E: E.dma_start(out=coefT_d, in_=coefT),
                      [(ve, m_ct)], inc=dG)
            if half == 0:
                tap("coefT", coefT, [(ve, m_ct)])
            m_sgbf = V(lambda E: E.tensor_copy(sc3[:, 0:8], statcol[:, 0:8]),
                       [(se, m_sge)])
            T(lambda E: E.matmul(B[7][0:8, 128:256], sc3[:, 0:8], ident[:],
                                 start=True, stop=True),
              [(ve, m_sgbf), (ve, m_ct)])
            m_sgt = pe.n
            m_sgte = V(lambda E: E.tensor_copy(sc3[0:8, 128:256],
                                               B[7][0:8, 128:256]), [(pe, m_sgt)])
            m_g = G(lambda E: E.dma_start(
                out=sgrow.rearrange("a (r n) -> a r n", r=8),
                in_=sc3[0:8, 128:256]), [(ve, m_sgte)], inc=dG)
            for n2 in range(2):
                waits = [(dG, m_g)]
                if bfree[2 + n2]:
                    waits.append(bfree[2 + n2])
                T(lambda E, n2=n2: E.matmul(B[2 + n2][:, :], ones_row[:],
                                            sgrow[0:1, n2 * 512:(n2 + 1) * 512],
                                            start=True, stop=True), waits)
            m_sgbm = pe.n
            m_sgbc = None
            for n2 in range(2):
                m_sgbc = V(lambda E, n2=n2: E.tensor_copy(
                    sgbc[:, n2 * 512:(n2 + 1) * 512], B[2 + n2][:, :]),
                    [(pe, m_sgbm)])
            bfree[2] = (ve, m_sgbc)
            bfree[3] = (ve, m_sgbc)

            MARKS.append((f"units{half}", pe.n, ve.n, se.n, dG.n))
            m_cbc_cur = None
            m_cm_prev = None
            for u in range(NUNITS):
                MARKS.append((f"h{half}u{u}", pe.n, ve.n, se.n, dG.n))
                shared = u < 4
                cvec = sgbc if shared else cbc
                m_coef_ok = (ve, m_sgbc) if shared else m_cbc_cur
                m_h_all = None
                for j in range(NPAIR):
                    sg_i = issue_up(up_all[u, 2 * j])
                    su_i = issue_up(up_all[u, 2 * j + 1])
                    qd = j % 2
                    bg = [0 + 4 * qd, 1 + 4 * qd]
                    bu = [2 + 4 * qd, 3 + 4 * qd]
                    for n2 in range(2):
                        b = bg[n2]
                        for k in range(KF):
                            waits = []
                            if k == 0:
                                waits = [up_arr[sg_i], (ve, m_xn3)]
                                if bfree[b]:
                                    waits.append(bfree[b])
                                if j == 0 and m_dn_all_prev:
                                    waits.append((pe, m_dn_all_prev))
                            T(lambda E, b=b, si=sg_i, k=k, n2=n2,
                              st=(k == 0), sp=(k == KF - 1): E.matmul(
                                B[b][:, :], wslot(si)[:, k * 128:(k + 1) * 128],
                                sa_t(k)[:, n2 * 512:(n2 + 1) * 512],
                                start=st, stop=sp), waits)
                    m_gch = pe.n
                    up_done[sg_i] = (pe, m_gch)
                    for n2 in range(2):
                        b = bu[n2]
                        for k in range(KF):
                            waits = []
                            if k == 0:
                                waits = [up_arr[su_i]]
                                if bfree[b]:
                                    waits.append(bfree[b])
                            T(lambda E, b=b, si=su_i, k=k, n2=n2,
                              st=(k == 0), sp=(k == KF - 1): E.matmul(
                                B[b][:, :], wslot(si)[:, k * 128:(k + 1) * 128],
                                sa_t(k)[:, n2 * 512:(n2 + 1) * 512],
                                start=st, stop=sp), waits)
                    m_uch = pe.n
                    up_done[su_i] = (pe, m_uch)
                    for n2 in range(2):
                        m_si = Sc(lambda E, b=bg[n2], n2=n2: E.activation(
                            sc1[:, n2 * 512:(n2 + 1) * 512], B[b][:, :], AF.Silu),
                            [(pe, m_gch)])
                        bfree[bg[n2]] = (se, m_si)
                        waits = [(se, m_si), (pe, m_uch)]
                        if m_coef_ok:
                            waits.append(m_coef_ok)
                        if j == 0 and n2 == 0 and m_acc_prev:
                            waits.append((ve, m_acc_prev))
                        V(lambda E, b=bu[n2], n2=n2: E.tensor_tensor(
                            sc2[:, n2 * 512:(n2 + 1) * 512],
                            sc1[:, n2 * 512:(n2 + 1) * 512], B[b][:, :], ALU.mult),
                            waits)
                        m_h = V(lambda E, j=j, n2=n2, cvec=cvec: E.tensor_tensor(
                            sa_t(j)[:, 1024 + n2 * 512:1024 + (n2 + 1) * 512],
                            sc2[:, n2 * 512:(n2 + 1) * 512],
                            cvec[:, n2 * 512:(n2 + 1) * 512], ALU.mult))
                        bfree[bu[n2]] = (ve, m_h)
                    m_h_all = ve.n
                if u == 0 and half == 0:
                    tap("h00", sa[:, 1024:2048], [(ve, m_h_all)])
                    tap("sgbc", sgbc[:], [(ve, m_sgbc)])
                dn_free = {0: None, 1: None}
                m_a = None
                dn_slabs = {}
                for mm in range(KF):
                    dn_slabs[mm] = issue_dn(dn_all[u, mm])
                    par = mm % 2
                    bb = [4 * par, 1 + 4 * par]
                    for n2 in range(2):
                        b = bb[n2]
                        for ki in range(DNK):
                            waits = []
                            if ki == 0:
                                waits = [dn_arr[dn_slabs[mm]]]
                                if mm <= 1:
                                    waits.append((ve, m_h_all))
                                    if bfree[b]:
                                        waits.append(bfree[b])
                                if dn_free[par]:
                                    waits.append(dn_free[par])
                            T(lambda E, b=b, si=dn_slabs[mm], ki=ki, n2=n2,
                              st=(ki == 0), sp=(ki == DNK - 1): E.matmul(
                                B[b][:, :], dslot(si)[:, ki * 128:(ki + 1) * 128],
                                sa_t(ki)[:, 1024 + n2 * 512:1024 + (n2 + 1) * 512],
                                start=st, stop=sp), waits)
                    m_dnch = pe.n
                    dn_done[dn_slabs[mm]] = (pe, m_dnch)
                    for n2 in range(2):
                        b = bb[n2]
                        if u == 0:
                            waits = [(pe, m_dnch)]
                            if mm == 0 and n2 == 0 and m_final_prev:
                                waits.append((ve, m_final_prev))
                            m_a = V(lambda E, mm=mm, n2=n2, b=b: E.tensor_copy(
                                acc_t(mm)[:, n2 * 512:(n2 + 1) * 512], B[b][:, :]),
                                waits)
                        else:
                            m_a = V(lambda E, mm=mm, n2=n2, b=b: E.tensor_tensor(
                                acc_t(mm)[:, n2 * 512:(n2 + 1) * 512],
                                acc_t(mm)[:, n2 * 512:(n2 + 1) * 512],
                                B[b][:, :], ALU.add), [(pe, m_dnch)])
                    dn_free[par] = (ve, m_a)
                for b in (0, 1):
                    bfree[b] = dn_free[0]
                for b in (4, 5):
                    bfree[b] = dn_free[1]
                m_dn_all_prev = pe.n
                m_acc_prev = ve.n
                if u == 0 and half == 0:
                    tap("accU0", ot[:, 0:4096], [(ve, m_acc_prev)])
                    tap("h05", sa[:, 5 * 2048 + 1024:5 * 2048 + 2048],
                        [(ve, m_acc_prev)])
                    tap("dnw0", dnring[:, 0:1408], [(ve, m_acc_prev)])
                if u == 3 and half == 0:
                    tap("accS", ot[:, 0:4096], [(ve, m_acc_prev)])
                if u == 4 and half == 0:
                    tap("cbc0", cbc[:], [(ve, m_acc_prev)])
                    tap("acc4", ot[:, 0:4096], [(ve, m_acc_prev)])
                nxt = u + 1
                if nxt < NUNITS and nxt >= 4:
                    e = nxt - 4
                    gw = [(dG, m_ctd), (pe, m_sgbm)]
                    if m_cm_prev:
                        gw.append((pe, m_cm_prev))
                    m_g2 = G(lambda E, e=e: E.dma_start(
                        out=rowstage.rearrange("a (r n) -> a r n", r=8),
                        in_=coefT_d[e:128:16, :]), gw, inc=dG)
                    for n2 in range(2):
                        waits = [(dG, m_g2)]
                        if bfree[2 + n2]:
                            waits.append(bfree[2 + n2])
                        T(lambda E, n2=n2: E.matmul(
                            B[2 + n2][:, :], ones_row[:],
                            rowstage[0:1, n2 * 512:(n2 + 1) * 512],
                            start=True, stop=True), waits)
                    m_cm = pe.n
                    m_cm_prev = m_cm
                    m_cbc = None
                    for n2 in range(2):
                        m_cbc = V(lambda E, n2=n2: E.tensor_copy(
                            cbc[:, n2 * 512:(n2 + 1) * 512], B[2 + n2][:, :]),
                            [(pe, m_cm)])
                    m_cbc_cur = (ve, m_cbc)
                    bfree[2] = (ve, m_cbc)
                    bfree[3] = (ve, m_cbc)
            m_units = ve.n
            m_units_pe_prev = pe.n
            if half == 0:
                tap("acc0", ot[:, 0:4096], [(ve, m_units)])

            MARKS.append((f"ln4_{half}", pe.n, ve.n, se.n, dG.n))
            m_pstat = 0
            for c in range(KF):
                m_sq = Sc(lambda E, c=c: E.activation(
                    sc3[:, 0:1024], acc_t(c), AF.Square),
                    [(ve, m_units), (pe, m_pstat)])
                m_cs = V(lambda E, c=c: E.tensor_copy(sc2[:, 0:1024], acc_t(c)),
                         [(ve, m_units), (pe, m_pstat)])
                for q2 in range(2):
                    ew = [(se, m_sq), (ve, m_cs)]
                    if c == 0:
                        for bq in (q2, 4 + q2):
                            if bfree[bq]:
                                ew.append(bfree[bq])
                    stat_mm(q2, c, KF, sc2[:, q2 * 512:(q2 + 1) * 512],
                            sc3[:, q2 * 512:(q2 + 1) * 512], ew)
                m_pstat = pe.n
            m_ln4bc = ln_finalize(2, t0)

            for c in range(KF):
                for q2 in range(2):
                    i, fr = next_xslot()
                    gq = t0 + q2 * 512
                    m_in = Sy(lambda E, i=i, c=c, gq=gq: E.dma_start(
                        out=xslot(i), in_=x2s[c * 128:(c + 1) * 128, gq:gq + 512]),
                        fr, inc=xsem[i])
                    V(lambda E, c=c, q2=q2, gq=gq: E.tensor_tensor(
                        f32tmp[:, :], acc_t(c)[:, q2 * 512:(q2 + 1) * 512],
                        bc0[:, gq:gq + 512], ALU.mult), [(ve, m_ln4bc)])
                    V(lambda E, gq=gq: E.tensor_tensor(
                        f32tmp[:, :], f32tmp[:, :], bc1[:, gq:gq + 512],
                        ALU.subtract))
                    m_o = V(lambda E, i=i: E.tensor_tensor(
                        xslot(i), f32tmp[:, :], xslot(i), ALU.add), [(xsem[i], m_in)])
                    m_wr2 = Sy(lambda E, i=i, c=c, gq=gq: E.dma_start(
                        out=out_fm[c * 128:(c + 1) * 128, gq:gq + 512],
                        in_=xslot(i)), [(ve, m_o)], inc=xsem[i])
                    xslot_free[i] = [(xsem[i], m_wr2)]
            m_final_prev = ve.n

        for i in range(XSLOTS):
            em.wait("scalar", xsem[i], xsem[i].n)
        em.wait("scalar", dO, dO.n)

        block.sync(lambda E: [f(E) for f in em.fns["sync"]])
        block.gpsimd(lambda E: [f(E) for f in em.fns["gpsimd"]])
        block.vector(lambda E: [f(E) for f in em.fns["vector"]])
        block.scalar(lambda E: [f(E) for f in em.fns["scalar"]])
        block.tensor(lambda E: [f(E) for f in em.fns["tensor"]])

    return nc


def _up_slabs(w):
    O, F = w.shape
    a = np.ascontiguousarray(w.T).reshape(F // 128, 128, O // 128, 128)
    return np.ascontiguousarray(a.transpose(2, 1, 0, 3)).reshape(O // 128, 128, F)


def prep_inputs(L0_attn_x, L0_attn_q1_w, L0_attn_k1_w, L0_attn_q2_w, L0_attn_k2_w,
                L0_attn_v_w, L0_attn_o_w, L0_attn_lambda_param, L0_ffn_gate_w,
                L0_ffn_sgate_w, L0_ffn_w13, L0_ffn_w2, L0_ffn_shared_gate_up_w,
                L0_ffn_shared_down_w):
    f32 = lambda a: np.ascontiguousarray(np.asarray(a), dtype=np.float32)
    qs = [_up_slabs(f32(w)) for w in (L0_attn_q1_w, L0_attn_k1_w,
                                      L0_attn_q2_w, L0_attn_k2_w, L0_attn_v_w)]
    o_s = _up_slabs(f32(L0_attn_o_w))
    aw = np.empty((5 * NH + KF, 128, 2048), np.float32)
    for h in range(NH):
        for wi in range(5):
            aw[5 * h + wi] = qs[wi][h]
    aw[5 * NH:] = o_s

    sgu = _up_slabs(f32(L0_ffn_shared_gate_up_w))
    sdw = _up_slabs(f32(L0_ffn_shared_down_w))
    w13 = f32(L0_ffn_w13)
    w2 = f32(L0_ffn_w2)
    up_all = np.empty((NUNITS, 2 * NPAIR, 128, 2048), np.float32)
    dn_all = np.empty((NUNITS, KF, 128, DNK * 128), np.float32)
    for g in range(4):
        for j in range(NPAIR):
            up_all[g, 2 * j] = sgu[11 * g + j]
            up_all[g, 2 * j + 1] = sgu[44 + 11 * g + j]
        dn_all[g] = sdw[:, :, 11 * g * 128:(11 * g + 11) * 128]
    for e in range(16):
        ue = _up_slabs(w13[e])
        de = _up_slabs(w2[e])
        for j in range(NPAIR):
            up_all[4 + e, 2 * j] = ue[j]
            up_all[4 + e, 2 * j + 1] = ue[NPAIR + j]
        dn_all[4 + e] = de

    gate_tt = np.ascontiguousarray(
        f32(L0_ffn_gate_w).reshape(16, KF, 128).transpose(2, 1, 0).reshape(128, KF * 16))
    sg_tt = np.ascontiguousarray(f32(L0_ffn_sgate_w).reshape(KF, 128).T)
    lam = f32(L0_attn_lambda_param).reshape(1, NH)

    shared = {"aw": aw, "up_all": up_all, "dn_all": dn_all,
              "gate_t": gate_tt, "sg_t": sg_tt, "lam_t": lam}
    x = f32(L0_attn_x)
    in_maps = []
    for b in range(NB):
        m = dict(shared)
        m["x_fm"] = np.ascontiguousarray(x[b].T)
        in_maps.append(m)
    return in_maps


def kernel(**inputs):
    in_maps = prep_inputs(**inputs)
    nc = build_nc()
    res = run_bass_kernel_spmd(nc, in_maps, core_ids=list(range(NB)))
    out = np.stack([np.ascontiguousarray(np.asarray(r["out_fm"]).T)
                    for r in res.results])
    return out.astype(np.float32)


# version 03
# speedup vs baseline: 1.0254x; 1.0254x over previous
import math
import numpy as np
from contextlib import ExitStack

import concourse.bass as bass
import concourse.mybir as mybir
from concourse.bass_utils import run_bass_kernel_spmd

F32 = mybir.dt.float32
BF16 = mybir.dt.bfloat16
AF = mybir.ActivationFunctionType
ALU = mybir.AluOpType
AX = mybir.AxisListType

D = 2048
S = 2048
KF = 16
NH = 16
HD = 128
NB = 8
HALF = 1024
QSCALE = 1.0 / math.sqrt(HD)
EPS = 1e-5
NUNITS = 20
NPAIR = 11
DNK = 11
WSLOTS = 3
DSLOTS = 3
XSLOTS = 3

ENGINES = ("sync", "gpsimd", "vector", "scalar", "tensor")
MARKS = []


class Ctr:
    __slots__ = ("h", "n", "eng")

    def __init__(self, h, eng=None):
        self.h = h
        self.n = 0
        self.eng = eng


class Em:

    def __init__(self):
        self.fns = {e: [] for e in ENGINES}
        self.waited = {e: {} for e in ENGINES}
        self.marks = {e: [] for e in ENGINES}

    def wait(self, eng, ctr, val):
        if ctr is None or val is None or val <= 0:
            return
        if ctr.eng == eng:
            return
        w = self.waited[eng]
        k = id(ctr)
        if w.get(k, 0) >= val:
            return
        w[k] = val
        self.fns[eng].append(lambda E, c=ctr, v=val: E.wait_ge(c.h, v))

    def op(self, eng, fn, waits=(), inc=None, inc_by=1):
        for cv in waits:
            if cv is None:
                continue
            self.wait(eng, cv[0], cv[1])
        if inc is not None:
            inc.n += inc_by
            self.marks[eng].append((inc.n, fn.__code__.co_firstlineno))
            self.fns[eng].append(
                lambda E, f=fn, c=inc, v=inc_by: f(E).then_inc(c.h, v))
            return inc.n
        self.fns[eng].append(fn)
        return None


EM_LAST = [None]


def build_nc(taps=()):
    nc = bass.Bass("TRN2", target_bir_lowering=False, debug=False)

    x_fm = nc.dram_tensor("x_fm", [D, S], F32, kind="ExternalInput").ap()
    aw = nc.dram_tensor("aw", [5 * NH + KF, 128, 2048], F32, kind="ExternalInput").ap()
    up_all = nc.dram_tensor("up_all", [NUNITS, 2 * NPAIR, 128, 2048], F32,
                            kind="ExternalInput").ap()
    dn_all = nc.dram_tensor("dn_all", [NUNITS, KF, 128, DNK * 128], F32,
                            kind="ExternalInput").ap()
    gate_t = nc.dram_tensor("gate_t", [128, KF * 16], F32, kind="ExternalInput").ap()
    sg_t = nc.dram_tensor("sg_t", [128, KF], F32, kind="ExternalInput").ap()
    lam_t = nc.dram_tensor("lam_t", [1, NH], F32, kind="ExternalInput").ap()
    x2s = nc.dram_tensor("x2s", [D, S], F32, kind="Internal").ap()
    coefT_d = nc.dram_tensor("coefT_d", [128, 128], BF16, kind="Internal").ap()
    out_fm = nc.dram_tensor("out_fm", [D, S], F32, kind="ExternalOutput").ap()
    tap_t = {}
    for name, shape, dt in taps:
        tap_t[name] = nc.dram_tensor("tap_" + name, list(shape), dt,
                                     kind="ExternalOutput").ap()

    em = Em()
    EM_LAST[0] = em
    with ExitStack() as ctx:
        sb = lambda nm, s, d: ctx.enter_context(nc.sbuf_tensor(nm, s, d))
        psb = lambda nm: ctx.enter_context(nc.psum_tensor(nm, [128, 512], F32))
        mksem = lambda nm, eng=None: Ctr(ctx.enter_context(nc.semaphore(nm)), eng)

        sa = sb("sa", [128, KF * 2048], BF16)
        ot = sb("ot", [128, KF * 2048], BF16)
        qkv = sb("qkv", [128, 8192], BF16)
        v_sb = sb("v_sb", [128, 2048], BF16)
        sc1 = sb("sc1", [128, 2048], BF16)
        sc2 = sb("sc2", [128, 2048], BF16)
        sc3 = sb("sc3", [128, 2048], BF16)
        pct = sb("pct", [128, 2048], BF16)
        wring = sb("wring", [128, WSLOTS * 2048], BF16)
        dnring = sb("dnring", [128, DSLOTS * DNK * 128], BF16)
        xring = sb("xring", [128, XSLOTS * 512], F32)
        bc0 = sb("bc0", [128, 2048], BF16)
        bc1 = sb("bc1", [128, 2048], BF16)
        cbc = sb("cbc", [128, 1024], BF16)
        sgbc = sb("sgbc", [128, 1024], BF16)
        gate_sb = sb("gate_sb", [128, KF * 16], F32)
        sg_sb = sb("sg_sb", [128, KF], BF16)
        lam_bc = sb("lam_bc", [128, NH], F32)
        ident = sb("ident", [128, 128], BF16)
        ones_row = sb("ones_row", [1, 128], BF16)
        ones_col = sb("ones_col", [128, 1], BF16)
        statcol = sb("statcol", [128, 32], F32)
        statrow = sb("statrow", [1, 16], F32)
        lamrow = sb("lamrow", [1, NH], F32)
        lamrow_bf = sb("lamrow_bf", [1, NH], BF16)
        nbias_bf = sb("nbias_bf", [1, 2], BF16)
        rowstage_t = sb("rowstage_t", [1, 1024], BF16)

        sa_t = lambda k: sa[:, k * 2048:(k + 1) * 2048]
        ot_t = lambda k: ot[:, k * 2048:(k + 1) * 2048]
        acc_t = lambda k: ot[:, k * 2048:(k + 1) * 2048].bitcast(F32)
        Q1 = qkv[:, 0:2048]
        K1 = qkv[:, 2048:4096]
        Q2 = qkv[:, 4096:6144]
        K2 = qkv[:, 6144:8192]
        f32tmp = qkv[:, 0:1024].bitcast(F32)
        rowA = qkv[0:1, 0:4096].bitcast(F32)
        rowB = qkv[0:1, 4096:8192].bitcast(F32)
        arow = sc1[0:1, :]
        brow = sc2[0:1, :]
        rl_f = pct[:, 0:256].bitcast(F32)
        rank_f = pct[:, 256:512].bitcast(F32)
        gtmp_f = pct[:, 512:768].bitcast(F32)
        coefbf = pct[:, 768:896]
        coefT = pct[:, 896:1024]
        rowstage = rowstage_t[0:1, :]
        sgrow = rowstage
        iota_f = sc1[:, 0:256].bitcast(F32)

        wslot = lambda i: wring[:, (i % WSLOTS) * 2048:(i % WSLOTS) * 2048 + 2048]
        dslot = lambda i: dnring[:, (i % DSLOTS) * 1408:(i % DSLOTS) * 1408 + 1408]
        xslot = lambda i: xring[:, (i % XSLOTS) * 512:(i % XSLOTS) * 512 + 512]

        B01 = [psb("pb%d" % i) for i in range(2)]
        PBIG = ctx.enter_context(nc.psum_tensor("pbig", [128, 2048], F32))
        B67 = [psb("pb%d" % i) for i in (6, 7)]
        B = [B01[0][:, :], B01[1][:, :],
             PBIG[:, 0:512], PBIG[:, 512:1024],
             PBIG[:, 1024:1536], PBIG[:, 1536:2048],
             B67[0][:, :], B67[1][:, :]]

        pe = mksem("pe", "tensor")
        ve = mksem("ve", "vector")
        se = mksem("se", "scalar")
        gp = mksem("gp", "gpsimd")
        rv = mksem("rv")
        dG = mksem("dG")
        dO = mksem("dO")
        s_lam = mksem("s_lam")
        s_gate = mksem("s_gate")
        s_sgw = mksem("s_sgw")
        wsem = [mksem("wsem%d" % i) for i in range(WSLOTS)]
        dsem = [mksem("dsem%d" % i) for i in range(DSLOTS)]
        xsem = [mksem("xsem%d" % i) for i in range(XSLOTS)]

        block = ctx.enter_context(nc.Block())

        T = lambda fn, waits=(), inc=pe: em.op("tensor", fn, waits, inc)
        V = lambda fn, waits=(), inc=ve: em.op("vector", fn, waits, inc)
        Sc = lambda fn, waits=(), inc=se: em.op("scalar", fn, waits, inc)
        G = lambda fn, waits=(), inc=None, inc_by=16: em.op("gpsimd", fn, waits, inc, inc_by)
        Sy = lambda fn, waits=(), inc=None, inc_by=16: em.op("sync", fn, waits, inc, inc_by)

        def tap(name, src_ap, waits=()):
            if name in tap_t:
                Sy(lambda E, o=tap_t[name], i=src_ap: E.dma_start(out=o, in_=i),
                   waits, inc=dO)

        bfree = {b: None for b in range(8)}
        m_pc_last = None

        m_iota = G(lambda E: E.iota(iota_f, pattern=[[1, 128]], base=0,
                                    channel_multiplier=-1,
                                    allow_small_or_imprecise_dtypes=True),
                   inc=gp, inc_by=1)
        V(lambda E: E.memset(ones_row[:], 1.0))
        V(lambda E: E.memset(ones_col[:], 1.0))
        m_id = V(lambda E: E.tensor_single_scalar(ident[:], iota_f, 0.0, ALU.is_equal),
                 [(gp, m_iota)])
        m_lam_in = Sy(lambda E: E.dma_start(out=lamrow[:], in_=lam_t), inc=s_lam)
        m_lamb = V(lambda E: E.tensor_copy(lamrow_bf[:], lamrow[:]), [(s_lam, m_lam_in)])
        m_lamm = T(lambda E: E.matmul(B[0][:, 0:NH], ones_row[:], lamrow_bf[:],
                                      start=True, stop=True), [(ve, m_lamb)])
        m_lam = V(lambda E: E.tensor_copy(lam_bc[:], B[0][:, 0:NH]), [(pe, m_lamm)])
        bfree[0] = (ve, m_lam)
        m_gate = G(lambda E: E.dma_start(out=gate_sb[:], in_=gate_t), inc=s_gate)
        m_sgw = G(lambda E: E.dma_start(out=sg_sb[:], in_=sg_t), inc=s_sgw)

        def stat_mm(nq, c, nchunks, sum_rhs, sq_rhs, extra_waits=()):
            st = (c == 0)
            sp = (c == nchunks - 1)
            T(lambda E, n=nq, r=sum_rhs, st=st, sp=sp:
              E.matmul(B[n][0:1, :], ones_col[:], r, start=st, stop=sp),
              extra_waits)
            T(lambda E, n=nq, r=sq_rhs, st=st, sp=sp:
              E.matmul(B[4 + n][0:1, :], ones_col[:], r, start=st, stop=sp))

        def ln_finalize(nquart, bc_lo):
            m_stats = pe.n
            for n in range(nquart):
                V(lambda E, n=n: E.tensor_scalar(
                    rowA[0:1, n * 512:(n + 1) * 512], B[n][0:1, :],
                    1.0 / D, None, ALU.mult), [(pe, m_stats)])
                V(lambda E, n=n: E.tensor_scalar(
                    rowB[0:1, n * 512:(n + 1) * 512], B[4 + n][0:1, :],
                    1.0 / D, None, ALU.mult))
            for n in range(nquart):
                V(lambda E, n=n: E.tensor_tensor(
                    B[n][0:1, :], rowA[0:1, n * 512:(n + 1) * 512],
                    rowA[0:1, n * 512:(n + 1) * 512], ALU.mult))
                V(lambda E, n=n: E.tensor_tensor(
                    rowB[0:1, n * 512:(n + 1) * 512],
                    rowB[0:1, n * 512:(n + 1) * 512], B[n][0:1, :], ALU.subtract))
            w = nquart * 512
            V(lambda E, w=w: E.tensor_scalar(rowB[0:1, 0:w], rowB[0:1, 0:w],
                                             EPS, None, ALU.add))
            m_var = ve.n
            Sc(lambda E, w=w: E.activation(rowB[0:1, 0:w], rowB[0:1, 0:w], AF.Ln,
                                           bias=0.0, scale=1.0), [(ve, m_var)])
            m_rstd = Sc(lambda E, w=w: E.activation(rowB[0:1, 0:w], rowB[0:1, 0:w],
                                                    AF.Exp, bias=0.0, scale=-0.5))
            V(lambda E, w=w: E.tensor_tensor(brow[0:1, 0:w], rowA[0:1, 0:w],
                                             rowB[0:1, 0:w], ALU.mult),
              [(se, m_rstd)])
            m_arow = V(lambda E, w=w: E.tensor_copy(arow[0:1, 0:w], rowB[0:1, 0:w]))
            for n in range(nquart):
                T(lambda E, n=n: E.matmul(B[n][:, :], ones_row[:],
                                          arow[0:1, n * 512:(n + 1) * 512],
                                          start=True, stop=True), [(ve, m_arow)])
                T(lambda E, n=n: E.matmul(B[4 + n][:, :], ones_row[:],
                                          brow[0:1, n * 512:(n + 1) * 512],
                                          start=True, stop=True))
            m_bcmm = pe.n
            m_last = None
            for n in range(nquart):
                V(lambda E, n=n, lo=bc_lo: E.tensor_copy(
                    bc0[:, lo + n * 512:lo + (n + 1) * 512], B[n][:, :]),
                    [(pe, m_bcmm)])
                m_last = V(lambda E, n=n, lo=bc_lo: E.tensor_copy(
                    bc1[:, lo + n * 512:lo + (n + 1) * 512], B[4 + n][:, :]))
            for n in range(nquart):
                bfree[n] = (ve, m_last)
                bfree[4 + n] = (ve, m_last)
            return m_last

        xuse = [0]
        xslot_free = [[] for _ in range(XSLOTS)]

        def next_xslot():
            i = xuse[0] % XSLOTS
            xuse[0] += 1
            return i, list(xslot_free[i])

        MARKS.append(("ln1", pe.n, ve.n, se.n, dG.n))
        m_pstat = 0
        for c in range(KF):
            m_pstat_next = pe.n
            for q in range(4):
                i, fr = next_xslot()
                m_in = Sy(lambda E, i=i, c=c, q=q: E.dma_start(
                    out=xslot(i),
                    in_=x_fm[c * 128:(c + 1) * 128, q * 512:(q + 1) * 512]), fr,
                    inc=xsem[i])
                m_cast = V(lambda E, i=i, c=c, q=q: E.tensor_copy(
                    sa_t(c)[:, q * 512:(q + 1) * 512], xslot(i)), [(xsem[i], m_in)])
                m_sq = Sc(lambda E, i=i, q=q: E.activation(
                    sc3[:, q * 512:(q + 1) * 512], xslot(i), AF.Square),
                    [(xsem[i], m_in), (pe, m_pstat)])
                xslot_free[i] = [(ve, m_cast), (se, m_sq)]
                ew = [(ve, m_cast), (se, m_sq)]
                if c == 0:
                    for bq in (q, 4 + q):
                        if bfree[bq]:
                            ew.append(bfree[bq])
                stat_mm(q, c, KF,
                        sa_t(c)[:, q * 512:(q + 1) * 512],
                        sc3[:, q * 512:(q + 1) * 512], ew)
            m_pstat = pe.n
        m_ln1bc = ln_finalize(4, 0)

        for c in range(KF):
            for q in range(4):
                i, fr = next_xslot()
                m_in = Sy(lambda E, i=i, c=c, q=q: E.dma_start(
                    out=xslot(i),
                    in_=x_fm[c * 128:(c + 1) * 128, q * 512:(q + 1) * 512]), fr,
                    inc=xsem[i])
                V(lambda E, i=i, q=q: E.tensor_tensor(
                    f32tmp[:, :], xslot(i), bc0[:, q * 512:(q + 1) * 512], ALU.mult),
                    [(xsem[i], m_in), (ve, m_ln1bc)])
                m_ap = V(lambda E, c=c, q=q: E.tensor_tensor(
                    sa_t(c)[:, q * 512:(q + 1) * 512], f32tmp[:, :],
                    bc1[:, q * 512:(q + 1) * 512], ALU.subtract))
                xslot_free[i] = [(ve, m_ap)]
        m_xn1 = ve.n
        tap("xn1", sa[:, 0:4096], [(ve, m_xn1)])

        NSLAB_A = 5 * NH + KF
        slab_done = {}
        slab_arr = {}
        issued = [0]

        def ensure_slab(s):
            while issued[0] <= min(s, NSLAB_A - 1):
                si = issued[0]
                prev = si - WSLOTS
                w = [slab_done[prev]] if prev >= 0 else []
                m = G(lambda E, si=si: E.dma_start(
                    out=wslot(si), in_=aw[si]), w, inc=wsem[si % WSLOTS])
                slab_arr[si] = (wsem[si % WSLOTS], m)
                issued[0] += 1

        ensure_slab(WSLOTS - 1)

        MARKS.append(("headloop", pe.n, ve.n, se.n, dG.n))
        bfree = {b: None for b in range(8)}

        for h in range(NH):
            MARKS.append((f"head{h}", pe.n, ve.n, se.n, dG.n))
            for wi, (dst, scale) in enumerate(
                    [(Q1, QSCALE), (K1, 1.0), (Q2, QSCALE), (K2, 1.0)]):
                s = 5 * h + wi
                ensure_slab(s + WSLOTS - 1)
                for n in range(4):
                    b = n % 2
                    for k in range(KF):
                        waits = []
                        if k == 0:
                            waits = [slab_arr[s], (ve, m_xn1)]
                            if bfree[b]:
                                waits.append(bfree[b])
                        T(lambda E, b=b, s=s, k=k, n=n, st=(k == 0), sp=(k == KF - 1):
                          E.matmul(B[b][:, :], wslot(s)[:, k * 128:(k + 1) * 128],
                                   sa_t(k)[:, n * 512:(n + 1) * 512],
                                   start=st, stop=sp), waits)
                    m_chain = pe.n
                    m_ev = Sc(lambda E, dst=dst, n=n, b=b, scale=scale: E.activation(
                        dst[:, n * 512:(n + 1) * 512], B[b][:, :], AF.Copy,
                        bias=0.0, scale=scale), [(pe, m_chain)])
                    bfree[b] = (se, m_ev)
                slab_done[s] = (pe, pe.n)

            s = 5 * h + 4
            ensure_slab(s + WSLOTS - 1)
            vwave = {}
            for m in range(16):
                b = m % 2
                off = ((m // 2) % 4) * 128
                for k in range(KF):
                    waits = []
                    if k == 0:
                        waits = [slab_arr[s]]
                        if m <= 1 and bfree[b]:
                            waits.append(bfree[b])
                        if m >= 2:
                            waits.append(vwave[m - 2])
                        if m >= 8:
                            waits.append(vwave[m - 8])
                    T(lambda E, b=b, off=off, s=s, k=k, m=m,
                      st=(k == 0), sp=(k == KF - 1):
                      E.matmul(B[b][:, off:off + 128],
                               sa_t(k)[:, m * 128:(m + 1) * 128],
                               wslot(s)[:, k * 128:(k + 1) * 128],
                               start=st, stop=sp), waits)
                m_chain = pe.n
                m_ev = Sc(lambda E, m=m, b=b, off=off: E.activation(
                    v_sb[:, m * 128:(m + 1) * 128], B[b][:, off:off + 128],
                    AF.Copy, bias=0.0, scale=1.0), [(pe, m_chain)])
                vwave[m] = (se, m_ev)
            bfree[0] = vwave[14]
            bfree[1] = vwave[15]
            slab_done[s] = (pe, pe.n)

            rbanks = [2, 3, 6, 7]
            m_bmm = 0
            for mi, (QA, KA) in enumerate([(Q1, K1), (Q2, K2)]):
                for xi, XA in enumerate([QA, KA]):
                    sqw = [(pe, m_bmm)]
                    if m_pc_last:
                        sqw.append((ve, m_pc_last))
                    m_sq = Sc(lambda E, XA=XA: E.activation(sc3[:, :], XA, AF.Square),
                              sqw)
                    for n in range(4):
                        waits = [(se, m_sq)]
                        if bfree[rbanks[n]]:
                            waits.append(bfree[rbanks[n]])
                        T(lambda E, n=n: E.matmul(
                            B[rbanks[n]][0:1, :], ones_col[:],
                            sc3[:, n * 512:(n + 1) * 512], start=True, stop=True),
                            waits)
                    m_mm = pe.n
                    m_bmm = m_mm
                    m_r4 = None
                    for n in range(4):
                        m_r4 = V(lambda E, n=n, xi=xi: E.reduce_max(
                            statrow[0:1, 4 * xi + n:4 * xi + n + 1],
                            B[rbanks[n]][0:1, :], axis=AX.X), [(pe, m_mm)],
                            inc=rv if n == 3 else ve)
                    m_nmax = V(lambda E, xi=xi: E.reduce_max(
                        statrow[0:1, 8 + xi:9 + xi], statrow[0:1, 4 * xi:4 * xi + 4],
                        axis=AX.X), [(rv, m_r4)], inc=rv)
                    em.wait("vector", rv, m_nmax)
                m_b2 = V(lambda E: E.tensor_tensor(
                    statrow[0:1, 10:11], statrow[0:1, 8:9],
                    statrow[0:1, 9:10], ALU.mult), [(rv, m_nmax)])
                m_ln = Sc(lambda E: E.activation(statrow[0:1, 10:11],
                                                 statrow[0:1, 10:11],
                                                 AF.Ln, bias=0.0, scale=1.0),
                          [(ve, m_b2)])
                V(lambda E: E.tensor_scalar(
                    statrow[0:1, 10:11], statrow[0:1, 10:11], 0.5,
                    math.log(1.02 / math.sqrt(HD)), ALU.mult, ALU.add),
                  [(se, m_ln)])
                m_vb = ve.n
                m_b = Sc(lambda E: E.activation(
                    statrow[0:1, 10:11], statrow[0:1, 10:11], AF.Exp,
                    bias=0.0, scale=1.0), [(ve, m_vb)])
                m_nb = V(lambda E, mi=mi: E.tensor_scalar(
                    nbias_bf[0:1, mi:mi + 1], statrow[0:1, 10:11], -1.0, None,
                    ALU.mult), [(se, m_b)])
                waits = [(ve, m_nb)]
                if bfree[2]:
                    waits.append(bfree[2])
                T(lambda E, mi=mi: E.matmul(
                    B[2][:, 256 + mi:257 + mi], ones_row[:],
                    nbias_bf[0:1, mi:mi + 1], start=True, stop=True), waits)
                m_bb = pe.n
                V(lambda E, mi=mi: E.tensor_copy(
                    statcol[:, 8 + mi:9 + mi], B[2][:, 256 + mi:257 + mi]),
                    [(pe, m_bb)])
            m_biases = ve.n
            if h == 0:
                tap("q1h0", Q1, [(se, se.n)])
                tap("vh0", v_sb[:], [(se, se.n)])

            def emit_scores(qt):
                par = qt % 2
                zoff = 16 * par
                E1t = sc1 if par == 0 else sc3
                E2t = sc2 if par == 0 else bc0
                for mi, (E_sb, bias_ix, zcol, lhsQ, rhsK) in enumerate((
                        (E1t, 8, zoff + 12, Q1, K1), (E2t, 9, zoff + 13, Q2, K2))):
                    for n in range(4):
                        waits = []
                        if n == 0:
                            if bfree[2]:
                                waits.append(bfree[2])
                            if qt == 0 and mi == 0:
                                waits.append((ve, m_biases))
                        T(lambda E, lhsQ=lhsQ, rhsK=rhsK, qt=qt, n=n:
                          E.matmul(PBIG[:, n * 512:(n + 1) * 512],
                                   lhsQ[:, qt * 128:(qt + 1) * 128],
                                   rhsK[:, n * 512:(n + 1) * 512],
                                   start=True, stop=True), waits)
                    m_s = pe.n
                    m_e = Sc(lambda E, E_sb=E_sb, bx=bias_ix, zc=zcol:
                             E.activation(E_sb[:, :], PBIG[:, :], AF.Exp,
                                          bias=statcol[:, bx:bx + 1], scale=1.0,
                                          accum_out=statcol[:, zc:zc + 1]),
                             [(pe, m_s)])
                    for bb_ in (2, 3, 4, 5):
                        bfree[bb_] = (se, m_e)
                m_exps = se.n
                V(lambda E, zoff=zoff: E.reciprocal(
                    statcol[:, zoff + 14:zoff + 15], statcol[:, zoff + 12:zoff + 13]),
                  [(se, m_exps)])
                m_rv = V(lambda E, zoff=zoff: E.reciprocal(
                    statcol[:, zoff + 15:zoff + 16], statcol[:, zoff + 13:zoff + 14]),
                    inc=rv)
                m_c2 = V(lambda E, h=h, zoff=zoff: E.tensor_tensor(
                    statcol[:, zoff + 15:zoff + 16], statcol[:, zoff + 15:zoff + 16],
                    lam_bc[:, h:h + 1], ALU.mult), [(rv, m_rv), (ve, m_lam)], inc=rv)
                for n in range(4):
                    V(lambda E, E2t=E2t, n=n, zoff=zoff: E.tensor_scalar(
                        E2t[:, n * 512:(n + 1) * 512], E2t[:, n * 512:(n + 1) * 512],
                        statcol[:, zoff + 15:zoff + 16], None, ALU.mult),
                        [(rv, m_c2)])
                    V(lambda E, E1t=E1t, n=n, zoff=zoff: E.tensor_scalar(
                        E1t[:, n * 512:(n + 1) * 512], E1t[:, n * 512:(n + 1) * 512],
                        statcol[:, zoff + 14:zoff + 15], None, ALU.mult))
                    m_pcn = V(lambda E, E1t=E1t, E2t=E2t, n=n: E.tensor_tensor(
                        E2t[:, n * 512:(n + 1) * 512], E2t[:, n * 512:(n + 1) * 512],
                        E1t[:, n * 512:(n + 1) * 512], ALU.subtract))
                    pc_marks[qt * 4 + n] = m_pcn
                return m_pcn

            def emit_tpv(qt):
                par = qt % 2
                E2t = sc2 if par == 0 else bc0
                pct_evs = []
                for wv in range(4):
                    b = 6 + (wv % 2)
                    for j in range(4):
                        jj = wv * 4 + j
                        waits = []
                        if j == 0:
                            waits = [(ve, pc_marks[qt * 4 + wv])]
                            if bfree[b]:
                                waits.append(bfree[b])
                        T(lambda E, b=b, jj=jj, j=j, E2t=E2t: E.matmul(
                            B[b][:, j * 128:(j + 1) * 128],
                            E2t[:, jj * 128:(jj + 1) * 128], ident[:],
                            start=True, stop=True), waits)
                    m_t = pe.n
                    m_evt = V(lambda E, b=b, wv=wv: E.tensor_copy(
                        pct[:, wv * 512:(wv + 1) * 512], B[b][:, :]), [(pe, m_t)])
                    bfree[b] = (ve, m_evt)
                    pct_evs.append((ve, m_evt))
                for j in range(16):
                    waits = []
                    if j == 0:
                        waits = list(pct_evs)
                        if bfree[6]:
                            waits.append(bfree[6])
                    T(lambda E, j=j, st=(j == 0), sp=(j == 15): E.matmul(
                        B[6][:, 0:128], v_sb[:, j * 128:(j + 1) * 128],
                        pct[:, j * 128:(j + 1) * 128], start=st, stop=sp), waits)
                m_pv = pe.n
                m_pve = V(lambda E, h=h, qt=qt: E.tensor_scalar(
                    ot_t(h)[:, qt * 128:(qt + 1) * 128], B[6][:, 0:128],
                    -1.0, None, ALU.mult), [(pe, m_pv)])
                bfree[6] = (ve, m_pve)

            pc_marks = {}
            emit_scores(0)
            for qt in range(1, 16):
                emit_scores(qt)
                emit_tpv(qt - 1)
            emit_tpv(15)
            m_pc_last = pc_marks[15 * 4 + 3]
            if h == 0:
                tap("oth0", ot[:, 0:2048], [(ve, ve.n)])

        MARKS.append(("o_proj", pe.n, ve.n, se.n, dG.n))
        for m in range(16):
            s = 5 * NH + m
            ensure_slab(s + WSLOTS - 1)
            banks = [0, 1, 2, 3] if (m % 2 == 0) else [4, 5, 6, 7]
            for n in range(4):
                bq = banks[n]
                for k in range(KF):
                    waits = []
                    if k == 0:
                        waits = [slab_arr[s]]
                        if bfree[bq]:
                            waits.append(bfree[bq])
                    T(lambda E, b=bq, s=s, k=k, n=n, st=(k == 0), sp=(k == KF - 1):
                      E.matmul(B[b][:, :], wslot(s)[:, k * 128:(k + 1) * 128],
                               ot_t(k)[:, n * 512:(n + 1) * 512],
                               start=st, stop=sp), waits)
            m_chain = pe.n
            for n in range(4):
                m_ev = V(lambda E, m=m, n=n, b=banks[n]: E.tensor_copy(
                    sa_t(m)[:, n * 512:(n + 1) * 512], B[b][:, :]), [(pe, m_chain)])
                bfree[banks[n]] = (ve, m_ev)
            slab_done[s] = (pe, pe.n)
        m_ao = ve.n
        tap("ao", sa[:, 0:4096], [(ve, m_ao)])

        MARKS.append(("ln2", pe.n, ve.n, se.n, dG.n))
        m_pstat = 0
        for c in range(KF):
            waits = [(ve, m_ao), (pe, m_pstat)]
            m_sq = Sc(lambda E, c=c: E.activation(sc3[:, :], sa_t(c), AF.Square),
                      waits)
            for q in range(4):
                ew = [(se, m_sq)]
                if c == 0:
                    for bq in (q, 4 + q):
                        if bfree[bq]:
                            ew.append(bfree[bq])
                stat_mm(q, c, KF, sa_t(c)[:, q * 512:(q + 1) * 512],
                        sc3[:, q * 512:(q + 1) * 512], ew)
            m_pstat = pe.n
        m_ln2bc = ln_finalize(4, 0)

        MARKS.append(("x2pass", pe.n, ve.n, se.n, dG.n))
        m_pstat = 0
        for c in range(KF):
            m_pstat_u = m_pstat
            for q in range(4):
                i, fr = next_xslot()
                m_in = Sy(lambda E, i=i, c=c, q=q: E.dma_start(
                    out=xslot(i),
                    in_=x_fm[c * 128:(c + 1) * 128, q * 512:(q + 1) * 512]), fr,
                    inc=xsem[i])
                V(lambda E, c=c, q=q: E.tensor_tensor(
                    f32tmp[:, :], sa_t(c)[:, q * 512:(q + 1) * 512],
                    bc0[:, q * 512:(q + 1) * 512], ALU.mult), [(ve, m_ln2bc)])
                V(lambda E, q=q: E.tensor_tensor(
                    f32tmp[:, :], f32tmp[:, :], bc1[:, q * 512:(q + 1) * 512],
                    ALU.subtract))
                m_x2 = V(lambda E, i=i: E.tensor_tensor(
                    xslot(i), f32tmp[:, :], xslot(i), ALU.add), [(xsem[i], m_in)])
                m_wr = Sy(lambda E, i=i, c=c, q=q: E.dma_start(
                    out=x2s[c * 128:(c + 1) * 128, q * 512:(q + 1) * 512],
                    in_=xslot(i)), [(ve, m_x2)], inc=xsem[i])
                m_cast = V(lambda E, i=i, c=c, q=q: E.tensor_copy(
                    sa_t(c)[:, q * 512:(q + 1) * 512], xslot(i)))
                m_sq = Sc(lambda E, i=i, q=q: E.activation(
                    sc3[:, q * 512:(q + 1) * 512], xslot(i), AF.Square),
                    [(ve, m_x2), (pe, m_pstat_u)])
                xslot_free[i] = [(xsem[i], m_wr), (ve, m_cast), (se, m_sq)]
                stat_mm(q, c, KF, sa_t(c)[:, q * 512:(q + 1) * 512],
                        sc3[:, q * 512:(q + 1) * 512],
                        [(ve, m_cast), (se, m_sq)])
            m_pstat = pe.n
        m_ln3bc = ln_finalize(4, 0)
        x2done = [(xsem[i], xsem[i].n) for i in range(XSLOTS)]

        up_issued = [issued[0]]
        up_arr = dict(slab_arr)
        up_done = dict(slab_done)

        def issue_up(src_ap):
            si = up_issued[0]
            prev = si - WSLOTS
            if prev >= 0:
                assert prev in up_done, f"up ring reuse without mark: {si}"
            w = [up_done[prev]] if prev >= 0 and prev in up_done else []
            m = G(lambda E, si=si, s=src_ap: E.dma_start(
                out=wslot(si), in_=s), w, inc=wsem[si % WSLOTS])
            up_arr[si] = (wsem[si % WSLOTS], m)
            up_issued[0] += 1
            return si

        dn_issued = [0]
        dn_arr = {}
        dn_done = {}

        def issue_dn(src_ap):
            si = dn_issued[0]
            prev = si - DSLOTS
            if prev >= 0:
                assert prev in dn_done, f"dn ring reuse without mark: {si}"
                w = [dn_done[prev]]
            else:
                w = []
            m = G(lambda E, si=si, s=src_ap: E.dma_start(
                out=dslot(si), in_=s), w, inc=dsem[si % DSLOTS])
            dn_arr[si] = (dsem[si % DSLOTS], m)
            dn_issued[0] += 1
            return si

        m_rtmm = None
        m_final_prev = None
        m_units_pe_prev = None
        m_dn_all_prev = None
        m_acc_prev = None

        for half in range(2):
            t0 = half * HALF
            MARKS.append((f"half{half}", pe.n, ve.n, se.n, dG.n))
            for c in range(KF):
                for q2 in range(2):
                    i, fr = next_xslot()
                    gq = t0 + q2 * 512
                    m_in = Sy(lambda E, i=i, c=c, gq=gq: E.dma_start(
                        out=xslot(i), in_=x2s[c * 128:(c + 1) * 128, gq:gq + 512]),
                        fr + x2done, inc=xsem[i])
                    waits = [(xsem[i], m_in), (ve, m_ln3bc)]
                    if m_units_pe_prev:
                        waits.append((pe, m_units_pe_prev))
                    if m_rtmm:
                        waits.append((pe, m_rtmm))
                    V(lambda E, i=i, gq=gq: E.tensor_tensor(
                        f32tmp[:, :], xslot(i), bc0[:, gq:gq + 512], ALU.mult),
                        waits)
                    m_ap = V(lambda E, gq=gq: E.tensor_tensor(
                        f32tmp[:, :], f32tmp[:, :],
                        bc1[:, gq:gq + 512], ALU.subtract))
                    m_cast3 = V(lambda E, c=c, q2=q2: E.tensor_copy(
                        sa_t(c)[:, q2 * 512:(q2 + 1) * 512], f32tmp[:, :]))
                    xslot_free[i] = [(ve, m_ap)]
                    for j in range(4):
                        tt = q2 * 4 + j
                        rwts = [(ve, m_ap)]
                        if c == 0:
                            rwts.append((s_gate, m_gate))
                            if bfree[tt]:
                                rwts.append(bfree[tt])
                        T(lambda E, c=c, j=j, tt=tt, st=(c == 0), sp=(c == KF - 1):
                          E.matmul(B[tt][:, 0:16],
                                   f32tmp[:, j * 128:(j + 1) * 128],
                                   gate_sb[:, c * 16:(c + 1) * 16],
                                   start=st, stop=sp), rwts)
                    m_rtmm = pe.n
            m_xn3 = ve.n
            if half == 0:
                tap("xn3", sa[:, 0:2048], [(ve, m_xn3)])

            MARKS.append((f"router{half}", pe.n, ve.n, se.n, dG.n))
            for tt in range(8):
                V(lambda E, tt=tt: E.tensor_copy(rl_f[:, tt * 16:(tt + 1) * 16],
                                                 B[tt][:, 0:16]), [(pe, m_rtmm)])
            m_rle0 = ve.n
            for tt in range(8):
                for k in range(KF):
                    waits = [(s_sgw, m_sgw)] if (tt == 0 and k == 0) else []
                    if tt == 0 and k == 0:
                        waits.append((ve, m_rle0))
                    T(lambda E, tt=tt, k=k, st=(k == 0), sp=(k == KF - 1): E.matmul(
                        B[6][:, 128 + tt:129 + tt],
                        sa_t(k)[:, tt * 128:(tt + 1) * 128],
                        sg_sb[:, k:k + 1], start=st, stop=sp), waits)
            m_sgl = pe.n
            m_rle = m_rle0
            m_sge = Sc(lambda E: E.activation(statcol[:, 0:8], B[6][:, 128:136],
                                              AF.Sigmoid), [(pe, m_sgl)])
            bfree[6] = (se, m_sge)

            V(lambda E: E.memset(rank_f, 0.0))
            for tt in range(8):
                sl = rl_f[:, tt * 16:(tt + 1) * 16]
                V(lambda E, sl=sl: E.reduce_max(statcol[:, 10:11], sl, axis=AX.X,
                                                negate=True), [(ve, m_rle)])
                m_ex = Sc(lambda E, sl=sl: E.activation(
                    sl, sl, AF.Exp, bias=statcol[:, 10:11], scale=1.0,
                    accum_out=statcol[:, 11:12]), [(ve, ve.n)])
                m_r = V(lambda E: E.reciprocal(statcol[:, 12:13], statcol[:, 11:12]),
                        [(se, m_ex)], inc=rv)
                m_rwv = V(lambda E, sl=sl: E.tensor_scalar(
                    sl, sl, statcol[:, 12:13], None, ALU.mult), [(rv, m_r)], inc=rv)
            m_rw = m_rwv
            rl3 = rl_f.rearrange("p (g e) -> p g e", e=16)
            rk3 = rank_f.rearrange("p (g e) -> p g e", e=16)
            gt3 = gtmp_f.rearrange("p (g e) -> p g e", e=16)
            for s_ in range(1, 16):
                w_ = 16 - s_
                m_g1 = V(lambda E, s_=s_, w_=w_: E.tensor_tensor(
                    gt3[:, :, 0:w_], rl3[:, :, s_:16], rl3[:, :, 0:w_], ALU.is_gt),
                    [(rv, m_rw)], inc=rv)
                V(lambda E, s_=s_, w_=w_: E.tensor_tensor(
                    rk3[:, :, 0:w_], rk3[:, :, 0:w_], gt3[:, :, 0:w_], ALU.add),
                    [(rv, m_g1)])
                m_g2 = V(lambda E, s_=s_, w_=w_: E.tensor_scalar(
                    gt3[:, :, 0:w_], gt3[:, :, 0:w_], -1.0, 1.0, ALU.mult, ALU.add),
                    inc=rv)
                m_g3 = V(lambda E, s_=s_, w_=w_: E.tensor_tensor(
                    rk3[:, :, s_:16], rk3[:, :, s_:16], gt3[:, :, 0:w_], ALU.add),
                    [(rv, m_g2)], inc=rv)
                em.wait("vector", rv, m_g3)
            m_rk = V(lambda E: E.tensor_single_scalar(rank_f, rank_f, 3.5, ALU.is_lt),
                     inc=rv)
            m_cf = V(lambda E: E.tensor_tensor(rank_f, rank_f, rl_f, ALU.mult),
                     [(rv, m_rk)], inc=rv)
            m_cbf = V(lambda E: E.tensor_copy(coefbf, rank_f), [(rv, m_cf)])
            waits = [(ve, m_cbf), (ve, m_id)]
            if bfree[7]:
                waits.append(bfree[7])
            T(lambda E: E.matmul(B[7][:, 0:128], coefbf, ident[:],
                                 start=True, stop=True), waits)
            m_ctm = pe.n
            m_ct = V(lambda E: E.tensor_copy(coefT, B[7][:, 0:128]), [(pe, m_ctm)])
            m_ctd = G(lambda E: E.dma_start(out=coefT_d, in_=coefT),
                      [(ve, m_ct)], inc=dG)
            if half == 0:
                tap("coefT", coefT, [(ve, m_ct)])
            m_sgbf = V(lambda E: E.tensor_copy(sc3[:, 0:8], statcol[:, 0:8]),
                       [(se, m_sge)])
            T(lambda E: E.matmul(B[7][0:8, 128:256], sc3[:, 0:8], ident[:],
                                 start=True, stop=True),
              [(ve, m_sgbf), (ve, m_ct)])
            m_sgt = pe.n
            m_sgte = V(lambda E: E.tensor_copy(sc3[0:8, 128:256],
                                               B[7][0:8, 128:256]), [(pe, m_sgt)])
            m_g = G(lambda E: E.dma_start(
                out=sgrow.rearrange("a (r n) -> a r n", r=8),
                in_=sc3[0:8, 128:256]), [(ve, m_sgte)], inc=dG)
            for n2 in range(2):
                waits = [(dG, m_g)]
                if bfree[2 + n2]:
                    waits.append(bfree[2 + n2])
                T(lambda E, n2=n2: E.matmul(B[2 + n2][:, :], ones_row[:],
                                            sgrow[0:1, n2 * 512:(n2 + 1) * 512],
                                            start=True, stop=True), waits)
            m_sgbm = pe.n
            m_sgbc = None
            for n2 in range(2):
                m_sgbc = V(lambda E, n2=n2: E.tensor_copy(
                    sgbc[:, n2 * 512:(n2 + 1) * 512], B[2 + n2][:, :]),
                    [(pe, m_sgbm)])
            bfree[2] = (ve, m_sgbc)
            bfree[3] = (ve, m_sgbc)

            MARKS.append((f"units{half}", pe.n, ve.n, se.n, dG.n))
            m_cbc_cur = None
            m_cm_prev = None
            for u in range(NUNITS):
                MARKS.append((f"h{half}u{u}", pe.n, ve.n, se.n, dG.n))
                shared = u < 4
                cvec = sgbc if shared else cbc
                m_coef_ok = (ve, m_sgbc) if shared else m_cbc_cur
                m_h_all = None
                for j in range(NPAIR):
                    sg_i = issue_up(up_all[u, 2 * j])
                    su_i = issue_up(up_all[u, 2 * j + 1])
                    qd = j % 2
                    bg = [0 + 4 * qd, 1 + 4 * qd]
                    bu = [2 + 4 * qd, 3 + 4 * qd]
                    for n2 in range(2):
                        b = bg[n2]
                        for k in range(KF):
                            waits = []
                            if k == 0:
                                waits = [up_arr[sg_i], (ve, m_xn3)]
                                if bfree[b]:
                                    waits.append(bfree[b])
                                if j == 0 and m_dn_all_prev:
                                    waits.append((pe, m_dn_all_prev))
                            T(lambda E, b=b, si=sg_i, k=k, n2=n2,
                              st=(k == 0), sp=(k == KF - 1): E.matmul(
                                B[b][:, :], wslot(si)[:, k * 128:(k + 1) * 128],
                                sa_t(k)[:, n2 * 512:(n2 + 1) * 512],
                                start=st, stop=sp), waits)
                    m_gch = pe.n
                    up_done[sg_i] = (pe, m_gch)
                    for n2 in range(2):
                        b = bu[n2]
                        for k in range(KF):
                            waits = []
                            if k == 0:
                                waits = [up_arr[su_i]]
                                if bfree[b]:
                                    waits.append(bfree[b])
                            T(lambda E, b=b, si=su_i, k=k, n2=n2,
                              st=(k == 0), sp=(k == KF - 1): E.matmul(
                                B[b][:, :], wslot(si)[:, k * 128:(k + 1) * 128],
                                sa_t(k)[:, n2 * 512:(n2 + 1) * 512],
                                start=st, stop=sp), waits)
                    m_uch = pe.n
                    up_done[su_i] = (pe, m_uch)
                    for n2 in range(2):
                        m_si = Sc(lambda E, b=bg[n2], n2=n2: E.activation(
                            sc1[:, n2 * 512:(n2 + 1) * 512], B[b][:, :], AF.Silu),
                            [(pe, m_gch)])
                        bfree[bg[n2]] = (se, m_si)
                        waits = [(se, m_si), (pe, m_uch)]
                        if m_coef_ok:
                            waits.append(m_coef_ok)
                        if j == 0 and n2 == 0 and m_acc_prev:
                            waits.append((ve, m_acc_prev))
                        V(lambda E, b=bu[n2], n2=n2: E.tensor_tensor(
                            sc2[:, n2 * 512:(n2 + 1) * 512],
                            sc1[:, n2 * 512:(n2 + 1) * 512], B[b][:, :], ALU.mult),
                            waits)
                        m_h = V(lambda E, j=j, n2=n2, cvec=cvec: E.tensor_tensor(
                            sa_t(j)[:, 1024 + n2 * 512:1024 + (n2 + 1) * 512],
                            sc2[:, n2 * 512:(n2 + 1) * 512],
                            cvec[:, n2 * 512:(n2 + 1) * 512], ALU.mult))
                        bfree[bu[n2]] = (ve, m_h)
                    m_h_all = ve.n
                if u == 0 and half == 0:
                    tap("h00", sa[:, 1024:2048], [(ve, m_h_all)])
                    tap("sgbc", sgbc[:], [(ve, m_sgbc)])
                dn_free = {0: None, 1: None}
                m_a = None
                dn_slabs = {}
                for mm in range(KF):
                    dn_slabs[mm] = issue_dn(dn_all[u, mm])
                    par = mm % 2
                    bb = [4 * par, 1 + 4 * par]
                    for n2 in range(2):
                        b = bb[n2]
                        for ki in range(DNK):
                            waits = []
                            if ki == 0:
                                waits = [dn_arr[dn_slabs[mm]]]
                                if mm <= 1:
                                    waits.append((ve, m_h_all))
                                    if bfree[b]:
                                        waits.append(bfree[b])
                                if dn_free[par]:
                                    waits.append(dn_free[par])
                            T(lambda E, b=b, si=dn_slabs[mm], ki=ki, n2=n2,
                              st=(ki == 0), sp=(ki == DNK - 1): E.matmul(
                                B[b][:, :], dslot(si)[:, ki * 128:(ki + 1) * 128],
                                sa_t(ki)[:, 1024 + n2 * 512:1024 + (n2 + 1) * 512],
                                start=st, stop=sp), waits)
                    m_dnch = pe.n
                    dn_done[dn_slabs[mm]] = (pe, m_dnch)
                    for n2 in range(2):
                        b = bb[n2]
                        if u == 0:
                            waits = [(pe, m_dnch)]
                            if mm == 0 and n2 == 0 and m_final_prev:
                                waits.append((ve, m_final_prev))
                            m_a = V(lambda E, mm=mm, n2=n2, b=b: E.tensor_copy(
                                acc_t(mm)[:, n2 * 512:(n2 + 1) * 512], B[b][:, :]),
                                waits)
                        else:
                            m_a = V(lambda E, mm=mm, n2=n2, b=b: E.tensor_tensor(
                                acc_t(mm)[:, n2 * 512:(n2 + 1) * 512],
                                acc_t(mm)[:, n2 * 512:(n2 + 1) * 512],
                                B[b][:, :], ALU.add), [(pe, m_dnch)])
                    dn_free[par] = (ve, m_a)
                for b in (0, 1):
                    bfree[b] = dn_free[0]
                for b in (4, 5):
                    bfree[b] = dn_free[1]
                m_dn_all_prev = pe.n
                m_acc_prev = ve.n
                if u == 0 and half == 0:
                    tap("accU0", ot[:, 0:4096], [(ve, m_acc_prev)])
                    tap("h05", sa[:, 5 * 2048 + 1024:5 * 2048 + 2048],
                        [(ve, m_acc_prev)])
                    tap("dnw0", dnring[:, 0:1408], [(ve, m_acc_prev)])
                if u == 3 and half == 0:
                    tap("accS", ot[:, 0:4096], [(ve, m_acc_prev)])
                if u == 4 and half == 0:
                    tap("cbc0", cbc[:], [(ve, m_acc_prev)])
                    tap("acc4", ot[:, 0:4096], [(ve, m_acc_prev)])
                nxt = u + 1
                if nxt < NUNITS and nxt >= 4:
                    e = nxt - 4
                    gw = [(dG, m_ctd), (pe, m_sgbm)]
                    if m_cm_prev:
                        gw.append((pe, m_cm_prev))
                    m_g2 = G(lambda E, e=e: E.dma_start(
                        out=rowstage.rearrange("a (r n) -> a r n", r=8),
                        in_=coefT_d[e:128:16, :]), gw, inc=dG)
                    for n2 in range(2):
                        waits = [(dG, m_g2)]
                        if bfree[2 + n2]:
                            waits.append(bfree[2 + n2])
                        T(lambda E, n2=n2: E.matmul(
                            B[2 + n2][:, :], ones_row[:],
                            rowstage[0:1, n2 * 512:(n2 + 1) * 512],
                            start=True, stop=True), waits)
                    m_cm = pe.n
                    m_cm_prev = m_cm
                    m_cbc = None
                    for n2 in range(2):
                        m_cbc = V(lambda E, n2=n2: E.tensor_copy(
                            cbc[:, n2 * 512:(n2 + 1) * 512], B[2 + n2][:, :]),
                            [(pe, m_cm)])
                    m_cbc_cur = (ve, m_cbc)
                    bfree[2] = (ve, m_cbc)
                    bfree[3] = (ve, m_cbc)
            m_units = ve.n
            m_units_pe_prev = pe.n
            if half == 0:
                tap("acc0", ot[:, 0:4096], [(ve, m_units)])

            MARKS.append((f"ln4_{half}", pe.n, ve.n, se.n, dG.n))
            m_pstat = 0
            for c in range(KF):
                m_sq = Sc(lambda E, c=c: E.activation(
                    sc3[:, 0:1024], acc_t(c), AF.Square),
                    [(ve, m_units), (pe, m_pstat)])
                m_cs = V(lambda E, c=c: E.tensor_copy(sc2[:, 0:1024], acc_t(c)),
                         [(ve, m_units), (pe, m_pstat)])
                for q2 in range(2):
                    ew = [(se, m_sq), (ve, m_cs)]
                    if c == 0:
                        for bq in (q2, 4 + q2):
                            if bfree[bq]:
                                ew.append(bfree[bq])
                    stat_mm(q2, c, KF, sc2[:, q2 * 512:(q2 + 1) * 512],
                            sc3[:, q2 * 512:(q2 + 1) * 512], ew)
                m_pstat = pe.n
            m_ln4bc = ln_finalize(2, t0)

            for c in range(KF):
                for q2 in range(2):
                    i, fr = next_xslot()
                    gq = t0 + q2 * 512
                    m_in = Sy(lambda E, i=i, c=c, gq=gq: E.dma_start(
                        out=xslot(i), in_=x2s[c * 128:(c + 1) * 128, gq:gq + 512]),
                        fr, inc=xsem[i])
                    V(lambda E, c=c, q2=q2, gq=gq: E.tensor_tensor(
                        f32tmp[:, :], acc_t(c)[:, q2 * 512:(q2 + 1) * 512],
                        bc0[:, gq:gq + 512], ALU.mult), [(ve, m_ln4bc)])
                    V(lambda E, gq=gq: E.tensor_tensor(
                        f32tmp[:, :], f32tmp[:, :], bc1[:, gq:gq + 512],
                        ALU.subtract))
                    m_o = V(lambda E, i=i: E.tensor_tensor(
                        xslot(i), f32tmp[:, :], xslot(i), ALU.add), [(xsem[i], m_in)])
                    m_wr2 = Sy(lambda E, i=i, c=c, gq=gq: E.dma_start(
                        out=out_fm[c * 128:(c + 1) * 128, gq:gq + 512],
                        in_=xslot(i)), [(ve, m_o)], inc=xsem[i])
                    xslot_free[i] = [(xsem[i], m_wr2)]
            m_final_prev = ve.n

        for i in range(XSLOTS):
            em.wait("scalar", xsem[i], xsem[i].n)
        em.wait("scalar", dO, dO.n)

        block.sync(lambda E: [f(E) for f in em.fns["sync"]])
        block.gpsimd(lambda E: [f(E) for f in em.fns["gpsimd"]])
        block.vector(lambda E: [f(E) for f in em.fns["vector"]])
        block.scalar(lambda E: [f(E) for f in em.fns["scalar"]])
        block.tensor(lambda E: [f(E) for f in em.fns["tensor"]])

    return nc


def _up_slabs(w):
    O, F = w.shape
    a = np.ascontiguousarray(w.T).reshape(F // 128, 128, O // 128, 128)
    return np.ascontiguousarray(a.transpose(2, 1, 0, 3)).reshape(O // 128, 128, F)


def prep_inputs(L0_attn_x, L0_attn_q1_w, L0_attn_k1_w, L0_attn_q2_w, L0_attn_k2_w,
                L0_attn_v_w, L0_attn_o_w, L0_attn_lambda_param, L0_ffn_gate_w,
                L0_ffn_sgate_w, L0_ffn_w13, L0_ffn_w2, L0_ffn_shared_gate_up_w,
                L0_ffn_shared_down_w):
    f32 = lambda a: np.ascontiguousarray(np.asarray(a), dtype=np.float32)
    qs = [_up_slabs(f32(w)) for w in (L0_attn_q1_w, L0_attn_k1_w,
                                      L0_attn_q2_w, L0_attn_k2_w, L0_attn_v_w)]
    o_s = _up_slabs(f32(L0_attn_o_w))
    aw = np.empty((5 * NH + KF, 128, 2048), np.float32)
    for h in range(NH):
        for wi in range(5):
            aw[5 * h + wi] = qs[wi][h]
    aw[5 * NH:] = o_s

    sgu = _up_slabs(f32(L0_ffn_shared_gate_up_w))
    sdw = _up_slabs(f32(L0_ffn_shared_down_w))
    w13 = f32(L0_ffn_w13)
    w2 = f32(L0_ffn_w2)
    up_all = np.empty((NUNITS, 2 * NPAIR, 128, 2048), np.float32)
    dn_all = np.empty((NUNITS, KF, 128, DNK * 128), np.float32)
    for g in range(4):
        for j in range(NPAIR):
            up_all[g, 2 * j] = sgu[11 * g + j]
            up_all[g, 2 * j + 1] = sgu[44 + 11 * g + j]
        dn_all[g] = sdw[:, :, 11 * g * 128:(11 * g + 11) * 128]
    for e in range(16):
        ue = _up_slabs(w13[e])
        de = _up_slabs(w2[e])
        for j in range(NPAIR):
            up_all[4 + e, 2 * j] = ue[j]
            up_all[4 + e, 2 * j + 1] = ue[NPAIR + j]
        dn_all[4 + e] = de

    gate_tt = np.ascontiguousarray(
        f32(L0_ffn_gate_w).reshape(16, KF, 128).transpose(2, 1, 0).reshape(128, KF * 16))
    sg_tt = np.ascontiguousarray(f32(L0_ffn_sgate_w).reshape(KF, 128).T)
    lam = f32(L0_attn_lambda_param).reshape(1, NH)

    shared = {"aw": aw, "up_all": up_all, "dn_all": dn_all,
              "gate_t": gate_tt, "sg_t": sg_tt, "lam_t": lam}
    x = f32(L0_attn_x)
    in_maps = []
    for b in range(NB):
        m = dict(shared)
        m["x_fm"] = np.ascontiguousarray(x[b].T)
        in_maps.append(m)
    return in_maps


def kernel(**inputs):
    in_maps = prep_inputs(**inputs)
    nc = build_nc()
    res = run_bass_kernel_spmd(nc, in_maps, core_ids=list(range(NB)))
    out = np.stack([np.ascontiguousarray(np.asarray(r["out_fm"]).T)
                    for r in res.results])
    return out.astype(np.float32)
